# version 16
# baseline (speedup 1.0000x reference)
import numpy as np
import ml_dtypes
import concourse.bass as bass
import concourse.mybir as mybir
from concourse.bass_utils import run_bass_kernel_spmd

F32 = mybir.dt.float32
BF16 = mybir.dt.bfloat16
F8 = mybir.dt.float8e4
AF = mybir.ActivationFunctionType
ALU = mybir.AluOpType
AXX = mybir.AxisListType.X

NCORE, T, D, H, DK, DV, CH, FI = 8, 2048, 2048, 8, 128, 256, 64, 8192
NCH = T // CH
EPS = 1e-5
C_EPS, C_M05, C_M1, C_P05 = 0, 1, 2, 3

ENGS = ("sync", "scalar", "vector", "tensor", "gpsimd")


class Ctr:
    def __init__(self, handle, name):
        self.h, self.name, self.n = handle, name, 0


class Prog:
    def __init__(self):
        self.q = {e: [] for e in ENGS}

    def op(self, eng, builder, waits=(), inc=None, inc_k=1):
        w = [(c, v) for (c, v) in waits
             if c is not None and v is not None and v > 0]
        if inc is not None:
            inc.n += inc_k
        self.q[eng].append((w, builder, inc, inc_k, inc.n if inc else None))
        return inc.n if inc is not None else None

    def replay(self, eng_name, eng, ser=None):
        n = 0
        for w, builder, inc, k, post in self.q[eng_name]:
            for c, v in w:
                eng.wait_ge(c.h, v)
            inst = builder(eng)
            if inc is not None:
                inst.then_inc(inc.h, k)
            if ser is not None and inst is not None:
                if inc is not None:
                    eng.wait_ge(inc.h, post)
                else:
                    n += 1
                    inst.then_inc(ser, 1)
                    eng.wait_ge(ser, n)


def build_program(debug=False, ser_engines=()):
    nc = bass.Bass("TRN2", target_bir_lowering=False, debug=False)

    x_h = nc.dram_tensor("x_own", [T, D], F32, kind="ExternalInput").ap()
    wq_h = nc.dram_tensor("Wq", [H * 128, 16 * 128], F32, kind="ExternalInput").ap()
    wk_h = nc.dram_tensor("Wk", [H * 128, 16 * 128], F32, kind="ExternalInput").ap()
    wvg_h = nc.dram_tensor("Wvg", [H * 128, 16 * 512], F32, kind="ExternalInput").ap()
    wgk1_h = nc.dram_tensor("Wgk1", [D, 16], F32, kind="ExternalInput").ap()
    wgk2_h = nc.dram_tensor("Wgk2", [16, 1024], F32, kind="ExternalInput").ap()
    bgk2_h = nc.dram_tensor("bgk2c", [128, H], F32, kind="ExternalInput").ap()
    wo_h = nc.dram_tensor("Wo", [4 * 128, 16 * 512], F32, kind="ExternalInput").ap()
    lnw_h = nc.dram_tensor("ln_w", [1, DV], F32, kind="ExternalInput").ap()
    lnb_h = nc.dram_tensor("ln_b", [1, DV], F32, kind="ExternalInput").ap()
    wfcT_h = nc.dram_tensor("wfcT", [D, FI], F32, kind="ExternalInput").ap()
    wprT_h = nc.dram_tensor("wprT", [FI, D], F32, kind="ExternalInput").ap()
    identb_h = nc.dram_tensor("identb", [128, 128], BF16, kind="ExternalInput").ap()
    identf_h = nc.dram_tensor("identf", [128, 128], F32, kind="ExternalInput").ap()
    m64_h = nc.dram_tensor("m64", [64, 64], F32, kind="ExternalInput").ap()
    cmask_h = nc.dram_tensor("cmask", [128, T], BF16, kind="ExternalInput").ap()
    consts_h = nc.dram_tensor("consts", [128, 8], F32, kind="ExternalInput").ap()
    onesrow_h = nc.dram_tensor("onesrow", [1, 128], F32, kind="ExternalInput").ap()
    smask_h = nc.dram_tensor("smask", [128, 8], F32, kind="ExternalInput").ap()
    y_h = nc.dram_tensor("y", [T, D], F32, kind="ExternalOutput").ap()

    d_xn = nc.dram_tensor("d_xn", [16 * 128, T], BF16).ap()
    d_og = nc.dram_tensor("d_og", [H * 128, 16 * DV], BF16).ap()
    d_qk = nc.dram_tensor("d_qk", [H * 128, T], BF16).ap()
    d_h = nc.dram_tensor("d_h", [64 * 4 * 128, 512], BF16).ap()
    d_w8p = nc.dram_tensor("d_w8p", [4 * 128, 64 * 512], F8).ap()
    cc1_src = nc.dram_tensor("cc1_src", [1, 128], F32).ap()
    cc1_out = nc.dram_tensor("cc1_out", [1, 128], F32).ap()
    cc2_src = nc.dram_tensor("cc2_src", [128, 2056], F32).ap()
    cc2_out = nc.dram_tensor("cc2_out", [NCORE * 128, 2056], F32).ap()
    cch_src = nc.dram_tensor("cch_src", [1, 128], F32).ap()
    cch_out = nc.dram_tensor("cch_out", [1, 128], F32).ap()
    d_sp = nc.dram_tensor("d_sp", [128, H * DV], F32).ap()
    d_sbf = nc.dram_tensor("d_sbf", [128, 16 * T], BF16).ap()
    if debug:
        t_one1 = nc.dram_tensor("t_one1", [1, 16], F32, kind="ExternalOutput").ap()
        t_scal = nc.dram_tensor("t_scal", [128, 12], F32, kind="ExternalOutput").ap()
        t_xn = nc.dram_tensor("t_xn", [16 * 128, D], F32, kind="ExternalOutput").ap()
        t_og = nc.dram_tensor("t_og", [H * 128, 16 * DV], F32, kind="ExternalOutput").ap()
        t_qk = nc.dram_tensor("t_qk", [H * 128, T], F32, kind="ExternalOutput").ap()
        t_h = nc.dram_tensor("t_h", [2 * 128, 512], F32, kind="ExternalOutput").ap()
        t_cc2 = nc.dram_tensor("t_cc2", [NCORE * 128, 2056], F32, kind="ExternalOutput").ap()
        t_sbf = nc.dram_tensor("t_sbf", [128, 16 * T], F32, kind="ExternalOutput").ap()
        t_sprev = nc.dram_tensor("t_sprev", [128, H * DV], F32, kind="ExternalOutput").ap()
        t_sdump = nc.dram_tensor("t_sdump", [128, 64], F32, kind="ExternalOutput").ap()

    from contextlib import ExitStack
    es = ExitStack()

    def mk_sb(stack):
        return lambda n, s, d: stack.enter_context(nc.sbuf_tensor(n, s, d))
    sb = mk_sb(es)
    psa = lambda n, s, d: es.enter_context(nc.psum_tensor(n, s, d))

    with es:
        consts = sb("consts_s", [128, 8], F32)
        onesrow = sb("onesrow_s", [1, 128], F32)
        identb = sb("identb_s", [128, 128], BF16)
        identf = sb("identf_s", [128, 128], F32)
        m64 = sb("m64_s", [64, 64], F32)
        cmask = sb("cmask_s", [128, T], BF16)
        smask = sb("smask_s", [128, 8], F32)
        lnwb = sb("lnwb", [128, DV], F32)
        lnbb = sb("lnbb", [128, DV], F32)
        lnw_row = sb("lnw_row", [1, DV], F32)
        lnb_row = sb("lnb_row", [1, DV], F32)
        bgcol = sb("bgcol", [128, H], F32)
        scal = sb("scal", [128, 12], F32)
        SC_RQX, SC_RQFC, SC_RQPR, SC_SQF, SC_C2, SC_C1, SC_SH, SC_SPR = range(8)
        one1 = sb("one1", [1, 16], F32)
        (O_XM, O_GX, O_SX, O_RQX, O_WFC, O_SFC, O_RQFC, O_WPR, O_SPR, O_RQPR,
         O_C1, O_RM, O_H2L, O_T0, O_T1, O_T2) = range(16)
        cc1row = sb("cc1row", [1, 128], F32)
        xmax_c = sb("xmax_c", [128, 4], F32)
        t1 = sb("t1", [16, T], BF16)
        stat_a = sb("stat_a", [128, 16], F32)
        stat_b = sb("stat_b", [128, 16], F32)
        stat_c = sb("stat_c", [128, 16], F32)
        red_t = sb("red_t", [128, 2], F32)
        ones256 = sb("ones256", [128, DV], F32)
        zero8 = sb("zero8", [128, 8], F32)
        h2col = sb("h2col", [128, 8], F32)
        wacc = sb("wacc", [1, 4], F32)
        wred = sb("wred", [1, 4], F32)
        sdump = sb("sdump", [128, 64], F32) if debug else None

        PB = [psa(f"pb{i}", [128, 512], F32) for i in range(4)]
        PBb = [psa(f"pbb{i}", [128, 1024], BF16) for i in range(2)]
        PB2 = [psa(f"pb2_{i}", [128, 512], F32) for i in range(2)]

        sems = [Ctr(es.enter_context(nc.semaphore(n)), n)
                for n in ("qS", "qC", "qG", "PE", "V", "A", "G", "CC")]
        qS, qC, qG, sPE, sV, sA, sG, sCC = sems

        def mksem(n):
            c = Ctr(es.enter_context(nc.semaphore(n)), n)
            sems.append(c)
            return c
        qX = [mksem("qX0"), mksem("qX1")]
        qSL = [mksem("qSL0"), mksem("qSL1")]
        qWL = [mksem("qWL0"), mksem("qWL1")]
        qXN = [mksem(f"qXN{i}") for i in range(4)]
        qW8 = mksem("qW8")
        qDH = [mksem("qDH0"), mksem("qDH1")]
        qSPo, qSPq, qSPs = mksem("qSPo"), mksem("qSPq"), mksem("qSPs")
        qC1 = mksem("qC1")
        qHM = mksem("qHM")
        qRO, qRQ = mksem("qRO"), mksem("qRQ")
        qWO = mksem("qWO")
        qGK1 = mksem("qGK1")
        qH6 = [mksem(f"qH6_{i}") for i in range(6)]
        qXF = [mksem("qXF0"), mksem("qXF1")]
        qY = [mksem("qY0"), mksem("qY1")]
        qSP2 = mksem("qSP2")
        qSB = [mksem("qSB0"), mksem("qSB1")]
        qSBL = [mksem("qSBL0"), mksem("qSBL1")]
        qW2 = [mksem("qW2_0"), mksem("qW2_1")]
        P = Prog()
        bankw = {}

        def bw(key):
            v = bankw.get(key)
            return v if v is not None else (None, None)

        for dst, src in ((consts, consts_h), (onesrow, onesrow_h),
                         (identb, identb_h), (identf, identf_h), (m64, m64_h),
                         (cmask, cmask_h), (smask, smask_h), (lnw_row, lnw_h),
                         (lnb_row, lnb_h), (bgcol, bgk2_h)):
            P.op("sync", lambda e, dst=dst, src=src: e.dma_start(
                out=dst[:], in_=src[:]), inc=qS, inc_k=16)
        init_qs = qS.n

        P.op("tensor", lambda e: e.matmul(PB2[0][:, 0:DV], onesrow[0:1, :],
                                          lnw_row[0:1, :], start=True, stop=True),
             waits=[(qS, init_qs)])
        n0 = P.op("tensor", lambda e: e.matmul(PB2[1][:, 0:DV], onesrow[0:1, :],
                                               lnb_row[0:1, :], start=True, stop=True),
                  inc=sPE)
        P.op("vector", lambda e: e.tensor_copy(lnwb[:], PB2[0][:, 0:DV]),
             waits=[(sPE, n0)])
        P.op("vector", lambda e: e.tensor_copy(lnbb[:], PB2[1][:, 0:DV]))
        P.op("vector", lambda e: e.memset(xmax_c[:], 0.0))
        P.op("vector", lambda e: e.memset(ones256[:], 1.0))
        zr_v = P.op("vector", lambda e: e.memset(zero8[:], 0.0), inc=sV)
        bankw["P2_0"] = (sV, zr_v)
        bankw["P2_1"] = (sV, zr_v)

        esW = es.enter_context(ExitStack())
        sbW = mk_sb(esW)
        wsl = [sbW("wsl0", [128, 16 * 256], F32), sbW("wsl1", [128, 16 * 256], F32)]
        xsl = [sbW("xsl0", [128, 16 * 512], BF16), sbW("xsl1", [128, 16 * 512], BF16)]
        w8q = sbW("w8q", [128, 2 * 2048], F8)
        wgk1s = sbW("wgk1s", [128, 16 * 16], BF16)
        wqs = sbW("wqs", [128, 16 * 128], BF16)
        wks = sbW("wks", [128, 16 * 128], BF16)
        wvgs = sbW("wvgs", [128, 16 * 512], BF16)
        wgk2s = sbW("wgk2s", [16, 128], BF16)

        esA = esW.enter_context(ExitStack())
        sbA = mk_sb(esA)
        xt = [sbA("xt0", [128, D], F32), sbA("xt1", [128, D], F32)]
        lnsc = sbA("lnsc", [128, D], F32)
        xnb = sbA("xnb", [128, D], BF16)
        stage = sbA("stage", [128, 16 * 512], BF16)
        sqd = sbA("sqd", [128, D], F32)

        ln_done, st_done, pe_tr_last, p0_vrb = {}, {}, {}, {}
        for tt in range(16):
            buf = xt[tt % 2]
            qs_x = P.op("sync", lambda e, buf=buf, tt=tt: e.dma_start(
                out=buf[:], in_=x_h[tt * 128:(tt + 1) * 128, :]),
                waits=[(sV, ln_done.get(tt - 2)), (qS, init_qs if tt < 2 else 0)],
                inc=qX[tt % 2], inc_k=16)
            xw = (qX[tt % 2], qs_x)
            a_sq = P.op("scalar", lambda e, buf=buf, tt=tt: e.activation(
                sqd[:], buf[:], AF.Square),
                waits=[xw, (sV, p0_vrb.get(tt - 1))], inc=sA)
            v_r1 = P.op("vector", lambda e, buf=buf, tt=tt: e.tensor_reduce(
                stat_a[:, tt:tt + 1], buf[:], AXX, ALU.add),
                waits=[xw], inc=sV)
            v_mu = P.op("vector", lambda e, tt=tt: e.tensor_scalar_mul(
                stat_a[:, tt:tt + 1], stat_a[:, tt:tt + 1], 1.0 / D),
                waits=[(sV, v_r1)], inc=sV)
            v_rb = P.op("vector", lambda e, tt=tt: e.tensor_reduce(
                stat_b[:, tt:tt + 1], sqd[:], AXX, ALU.add),
                waits=[(sA, a_sq)], inc=sV)
            p0_vrb[tt] = v_rb
            v_ms = P.op("vector", lambda e, tt=tt: e.tensor_scalar_mul(
                stat_b[:, tt:tt + 1], stat_b[:, tt:tt + 1], 1.0 / D),
                waits=[(sV, v_rb)], inc=sV)
            v_m2 = P.op("vector", lambda e, tt=tt: e.tensor_scalar_mul(
                stat_c[:, tt:tt + 1], stat_a[:, tt:tt + 1],
                stat_a[:, tt:tt + 1]), waits=[(sV, v_mu)], inc=sV)
            v_var = P.op("vector", lambda e, tt=tt: e.tensor_sub(
                stat_c[:, tt:tt + 1], stat_b[:, tt:tt + 1], stat_c[:, tt:tt + 1]),
                waits=[(sV, v_ms), (sV, v_m2)], inc=sV)
            if debug:
                P.op("vector", lambda e, tt=tt: e.tensor_copy(
                    sdump[:, 4 * tt:4 * tt + 1], stat_a[:, tt:tt + 1]))
                P.op("vector", lambda e, tt=tt: e.tensor_copy(
                    sdump[:, 4 * tt + 1:4 * tt + 2], stat_b[:, tt:tt + 1]))
                P.op("vector", lambda e, tt=tt: e.tensor_copy(
                    sdump[:, 4 * tt + 2:4 * tt + 3], stat_c[:, tt:tt + 1]))
            P.op("scalar", lambda e, tt=tt: e.activation(
                stat_c[:, tt:tt + 1], stat_c[:, tt:tt + 1], AF.Ln,
                bias=consts[:, C_EPS:C_EPS + 1]), waits=[(sV, v_var)])
            a_rs = P.op("scalar", lambda e, tt=tt: e.activation(
                stat_c[:, tt:tt + 1], stat_c[:, tt:tt + 1], AF.Exp,
                scale=consts[:, C_M05:C_M05 + 1]), inc=sA)
            if debug:
                P.op("vector", lambda e, tt=tt: e.tensor_copy(
                    sdump[:, 4 * tt + 3:4 * tt + 4], stat_c[:, tt:tt + 1]),
                    waits=[(sA, a_rs)])
            v_ln = P.op("vector", lambda e, buf=buf, tt=tt: e.tensor_scalar(
                lnsc[:], buf[:], stat_a[:, tt:tt + 1], stat_c[:, tt:tt + 1],
                ALU.subtract, ALU.mult), waits=[(sA, a_rs)], inc=sV)
            v_r2 = P.op("vector", lambda e, tt=tt: e.tensor_reduce(
                stat_b[:, tt:tt + 1], lnsc[:], AXX, ALU.max,
                apply_absolute_value=True), waits=[(sV, v_ln)], inc=sV)
            P.op("vector", lambda e, tt=tt: e.tensor_max(
                xmax_c[:, 0:1], xmax_c[:, 0:1], stat_b[:, tt:tt + 1]),
                waits=[(sV, v_r2)])
            v_xnb = P.op("vector", lambda e: e.tensor_copy(xnb[:], lnsc[:]),
                         waits=[(sPE, pe_tr_last.get(tt - 1)), (sV, v_ln)],
                         inc=sV)
            st = tt % 4
            for kb in range(16):
                key = f"Pb_{kb % 2}"
                pe_tr = P.op("tensor", lambda e, kb=kb: e.transpose(
                    PBb[kb % 2][:, 0:128], xnb[:, kb * 128:(kb + 1) * 128],
                    identb[:]), waits=[(sV, v_xnb), bw(key)], inc=sPE)
                pe_tr_last[tt] = pe_tr
                bankw[key] = (sV, P.op(
                    "vector", lambda e, kb=kb, st=st: e.tensor_copy(
                        stage[:, kb * 512 + st * 128:kb * 512 + (st + 1) * 128],
                        PBb[kb % 2][:, 0:128]),
                    waits=[(sPE, pe_tr)] +
                    ([(qXN[tt // 4 - 1], st_done.get(tt - 1))]
                     if (kb == 0 and st == 0 and tt >= 4)
                     else []), inc=sV))
            ln_done[tt] = sV.n
            if st == 3:
                g = tt // 4
                st_done[g * 4 + 3] = P.op("sync", lambda e, g=g: e.dma_start(
                    out=d_xn.rearrange("(kb p) t -> p kb t", p=128)
                    [:, :, g * 512:(g + 1) * 512],
                    in_=stage[:].rearrange("p (kb q) -> p kb q", kb=16)),
                    waits=[(sV, ln_done[tt])], inc=qXN[g], inc_k=16)
        p0_qs = qS.n
        p0_v = sV.n

        xch = {}

        def emit_xmax_launch():
            pe_xm = P.op("tensor", lambda e: e.transpose(
                PB2[0][0:1, 0:128], xmax_c[:, 0:1], identf[:]),
                waits=[(sV, p0_v), bw("P2_0")], inc=sPE)
            v_xm = P.op("vector", lambda e: e.tensor_reduce(
                one1[0:1, O_XM:O_XM + 1], PB2[0][0:1, 0:128], AXX, ALU.max),
                waits=[(sPE, pe_xm)], inc=sV)
            bankw["P2_0"] = (sV, v_xm)
            v_cc1 = P.op("vector", lambda e: e.tensor_scalar_mul(
                cc1row[0:1, :], onesrow[0:1, :], one1[0:1, O_XM:O_XM + 1]),
                waits=[(sV, v_xm)], inc=sV)
            qg_cc1 = P.op("gpsimd", lambda e: e.dma_start(
                out=cc1_src[:], in_=cc1row[:]),
                waits=[(sV, v_cc1)], inc=qC1, inc_k=16)
            xch["cc1_n"] = P.op("gpsimd", lambda e: e.collective_compute(
                "AllReduce", ALU.max, replica_groups=[list(range(NCORE))],
                ins=[cc1_src.opt()], outs=[cc1_out.opt()]),
                waits=[(qC1, qg_cc1)], inc=sCC)

        def emit_xmax_tail():
            qg_gx = P.op("gpsimd", lambda e: e.dma_start(
                out=one1[0:1, O_GX:O_GX + 1], in_=cc1_out[0:1, 0:1]),
                waits=[(sCC, xch["cc1_n"])], inc=qC1, inc_k=16)
            v_sx = P.op("vector", lambda e: e.tensor_scalar(
                one1[0:1, O_SX:O_SX + 1], one1[0:1, O_GX:O_GX + 1],
                1.0 / 448.0, 1e-6, ALU.mult, ALU.max),
                waits=[(qC1, qg_gx)], inc=sV)
            v_t0 = P.op("vector", lambda e: e.tensor_scalar_mul(
                one1[0:1, O_T0:O_T0 + 1], one1[0:1, O_SX:O_SX + 1], 2.0),
                waits=[(sV, v_sx)], inc=sV)
            v_rq0 = P.op("vector", lambda e: e.reciprocal(
                one1[0:1, O_RQX:O_RQX + 1], one1[0:1, O_T0:O_T0 + 1]),
                waits=[(sV, v_t0)], inc=sV)
            xch["v_sx"] = v_sx
            xch["v_rqx"] = bcast_col(O_RQX, SC_RQX, v_rq0)

        def bcast_col(src_col, dst_col, wait_v):
            pe = P.op("tensor", lambda e, s=src_col: e.matmul(
                PB2[1][:, 0:1], onesrow[0:1, :], one1[0:1, s:s + 1],
                start=True, stop=True), waits=[(sV, wait_v), bw("P2_1")], inc=sPE)
            v = P.op("vector", lambda e, d=dst_col: e.tensor_copy(
                scal[:, d:d + 1], PB2[1][:, 0:1]), waits=[(sPE, pe)], inc=sV)
            bankw["P2_1"] = (sV, v)
            return v

        esA.close()

        wfcT_r = wfcT_h.rearrange("(a p) j -> p a j", p=128)
        wprT_r = wprT_h.rearrange("(a p) c -> p a c", p=128)
        d_w8p_st = d_w8p.rearrange("(c p) (mt q) -> p c mt q", p=128, q=512)
        wm_state = {"seq": 0, "cons": {}, "wpr_g": None, "wfc_g": None,
                    "w8_qc": {}, "p55_v": None, "vals": {}}

        def emit_wmax_slab(i):
            seq = wm_state["seq"]; wm_state["seq"] += 1
            buf = wsl[seq % 2]

            def ld(e, i=i, buf=buf):
                if i < 32:
                    return e.dma_start(
                        out=buf[:].rearrange("p (a j) -> p a j", a=16),
                        in_=wfcT_r[:, :, i * 256:(i + 1) * 256])
                return e.dma_start(
                    out=buf[:].rearrange("p (a j) -> p a j", a=2),
                    in_=wprT_r[:, (i - 32) * 2:(i - 31) * 2, :])
            qc_w = P.op("scalar", ld, waits=[(sV, wm_state["cons"].get(seq - 2))],
                        inc=qWL[seq % 2], inc_k=16)
            v_r3 = P.op("vector", lambda e, buf=buf: e.tensor_reduce(
                red_t[:, 0:1], buf[:], AXX, ALU.max, apply_absolute_value=True),
                waits=[(qWL[seq % 2], qc_w), (sV, zr_v)], inc=sV)
            col = 1 if i < 32 else 2
            g = P.op("vector", lambda e, col=col: e.tensor_max(
                xmax_c[:, col:col + 1], xmax_c[:, col:col + 1], red_t[:, 0:1]),
                waits=[(sV, v_r3)], inc=sV)
            wm_state["cons"][seq] = g
            if i == 31:
                wm_state["wfc_g"] = g
            if i == 63:
                wm_state["wpr_g"] = g

        def emit_wscale_tail(which):
            if which == "wpr":
                colx, omax, osc, orq, dcol, gv = 2, O_WPR, O_SPR, O_RQPR, SC_RQPR, wm_state["wpr_g"]
            else:
                colx, omax, osc, orq, dcol, gv = 1, O_WFC, O_SFC, O_RQFC, SC_RQFC, wm_state["wfc_g"]
            pe_w = P.op("tensor", lambda e, colx=colx: e.transpose(
                PB2[0][0:1, 0:128], xmax_c[:, colx:colx + 1], identf[:]),
                waits=[(sV, gv), bw("P2_0")], inc=sPE)
            v_w = P.op("vector", lambda e, omax=omax: e.tensor_reduce(
                one1[0:1, omax:omax + 1], PB2[0][0:1, 0:128], AXX, ALU.max),
                waits=[(sPE, pe_w)], inc=sV)
            bankw["P2_0"] = (sV, v_w)
            v_sc = P.op("vector", lambda e, omax=omax, osc=osc: e.tensor_scalar(
                one1[0:1, osc:osc + 1], one1[0:1, omax:omax + 1],
                1.0 / 448.0, 1e-6, ALU.mult, ALU.max), waits=[(sV, v_w)], inc=sV)
            v_t1 = P.op("vector", lambda e, osc=osc: e.tensor_scalar_mul(
                one1[0:1, O_T1:O_T1 + 1], one1[0:1, osc:osc + 1], 2.0),
                waits=[(sV, v_sc)], inc=sV)
            v_r = P.op("vector", lambda e, orq=orq: e.reciprocal(
                one1[0:1, orq:orq + 1], one1[0:1, O_T1:O_T1 + 1]),
                waits=[(sV, v_t1)], inc=sV)
            wm_state["vals"][dcol] = bcast_col(orq, dcol, v_r)
            if which == "wpr":
                wm_state["vals"][SC_SPR] = bcast_col(O_SPR, SC_SPR, v_r)
            else:
                v_c1a = P.op("vector", lambda e: e.tensor_scalar_mul(
                    one1[0:1, O_C1:O_C1 + 1], one1[0:1, O_SX:O_SX + 1], 4.0),
                    waits=[(sV, xch["v_rqx"]), (sV, xch["v_sx"])], inc=sV)
                v_c1 = P.op("vector", lambda e: e.tensor_mul(
                    one1[0:1, O_C1:O_C1 + 1], one1[0:1, O_C1:O_C1 + 1],
                    one1[0:1, O_SFC:O_SFC + 1]),
                    waits=[(sV, v_c1a), (sV, v_sc)], inc=sV)
                wm_state["vals"]["v_c1"] = v_c1
                wm_state["vals"][SC_C1] = bcast_col(O_C1, SC_C1, v_c1)

        def emit_p55_slab(msl):
            seq = wm_state["seq"]; wm_state["seq"] += 1
            buf = wsl[seq % 2]
            qc_w = P.op("scalar", lambda e, msl=msl, buf=buf: e.dma_start(
                out=buf[:].rearrange("p (a j) -> p a j", a=2),
                in_=wprT_r[:, msl * 2:(msl + 1) * 2, :]),
                waits=[(sV, wm_state["cons"].get(seq - 2))],
                inc=qWL[seq % 2], inc_k=16)
            v_q = P.op("vector", lambda e, buf=buf: e.tensor_scalar_mul(
                w8q[:].rearrange("p (c a q) -> p a c q", c=4, a=2),
                buf[:].rearrange("p (a c q) -> p a c q", a=2, c=4),
                scal[:, SC_RQPR:SC_RQPR + 1]),
                waits=[(qWL[seq % 2], qc_w), (sV, wm_state["vals"][SC_RQPR]),
                       (qW8, wm_state["w8_qc"].get(msl - 1))], inc=sV)
            wm_state["cons"][seq] = v_q
            wm_state["p55_v"] = v_q
            wm_state["w8_qc"][msl] = P.op("scalar", lambda e, msl=msl: e.dma_start(
                out=d_w8p_st[:, :, msl * 2:(msl + 1) * 2, :],
                in_=w8q[:].rearrange("p (c a q) -> p c a q", c=4, a=2)),
                waits=[(sV, v_q)], inc=qW8, inc_k=16)

        qg_wgk1 = P.op("gpsimd", lambda e: e.dma_start(
            out=wgk1s[:].rearrange("p (a c) -> p a c", a=16),
            in_=wgk1_h.rearrange("(a p) c -> p a c", p=128)), inc=qGK1, inc_k=16)

        d_xn_ld = d_xn.rearrange("(kb p) t -> p kb t", p=128)
        xsl_seq = [0]

        def load_xsl(ts4):
            seq = xsl_seq[0]
            buf = xsl[seq % 2]
            g = bw(("xsl", seq % 2))
            qv = P.op("sync", lambda e, ts4=ts4, buf=buf: e.dma_start(
                out=buf[:].rearrange("p (kb q) -> p kb q", kb=16),
                in_=d_xn_ld[:, :, ts4 * 512:(ts4 + 1) * 512]),
                waits=[(qXN[ts4], st_done[ts4 * 4 + 3]), g],
                inc=qSL[seq % 2], inc_k=16)
            xsl_seq[0] += 1
            return seq, buf, (qSL[seq % 2], qv)

        def mark_xsl(seq, ctr, val):
            bankw[("xsl", seq % 2)] = (ctr, val)

        t1_v = None
        for ts4 in range(4):
            seq, buf, qs_x = load_xsl(ts4)

            def mmt1(e, buf=buf, ts4=ts4):
                for kb in range(16):
                    last = e.matmul(PB[ts4 % 2][0:16, 0:512],
                                    wgk1s[:, kb * 16:(kb + 1) * 16],
                                    buf[:, kb * 512:(kb + 1) * 512],
                                    start=(kb == 0), stop=(kb == 15))
                return last
            pe_t1 = P.op("tensor", mmt1,
                         waits=[qs_x, (qGK1, qg_wgk1), bw(f"P{ts4 % 2}")],
                         inc=sPE)
            mark_xsl(seq, sPE, pe_t1)
            t1_v = P.op("vector", lambda e, ts4=ts4: e.tensor_copy(
                t1[:, ts4 * 512:(ts4 + 1) * 512], PB[ts4 % 2][0:16, 0:512]),
                waits=[(sPE, pe_t1)], inc=sV)
            bankw[f"P{ts4 % 2}"] = (sV, t1_v)

        esB = esW.enter_context(ExitStack())
        sbB = mk_sb(esB)
        qT = sbB("qT", [128, T], BF16)
        kT = sbB("kT", [128, T], BF16)
        vg = sbB("vg", [128, 16 * 512], BF16)
        scrA = sbB("scrA", [128, T], F32)
        scrB = sbB("scrB", [128, T], F32)
        gkT = sbB("gkT", [128, T], BF16)
        e2b = sbB("e2b", [128, T], BF16)
        sqt = sbB("sqt", [128, T], BF16)
        skt = sbB("skt", [128, T], BF16)
        qtp = sbB("qtp", [128, 2 * CH], BF16)
        qkl = sbB("qkl", [128, T], BF16)
        k2a = sbB("k2a", [128, 16 * 128], BF16)
        dall = sbB("dall", [128, 128 * NCH], BF16)
        sall = sbB("sall", [128, 128 * NCH], BF16)
        ebz = sbB("ebz", [128, 128 * NCH], BF16)
        scp = sbB("scp", [128, 2 * DV], BF16)
        pse = sbB("pse", [64, 2 * 64], BF16)
        cscol = sbB("cscol", [128, 16], F32)
        ogasm = sbB("ogasm", [128, 16 * DV], BF16)
        eball = sbB("eball", [128, NCH], F32)
        cpall = sbB("cpall", [128, NCH], F32)
        kapall = sbB("kapall", [128, NCH], F32)
        onesc = sbB("onesc", [128, NCH], F32)
        sfin = sbB("sfin", [128, DV], F32)

        dall3 = dall[:].rearrange("p (e i) -> p e i", i=NCH)
        sall3 = sall[:].rearrange("p (e i) -> p e i", i=NCH)
        ebz3 = ebz[:].rearrange("p (e i) -> p e i", i=NCH)
        Bc3 = scrB[:].rearrange("p (i c) -> p i c", c=CH)
        d_og_r = d_og.rearrange("(h p) f -> h p f", p=128)
        d_qk_r = d_qk.rearrange("(h p) f -> h p f", p=128)

        v_ones = P.op("vector", lambda e: e.memset(onesc[:], 1.0), inc=sV)

        head_pe, a_abs = {}, {}
        spill_og, spill_qk, spill_sf = {}, {}, {}
        head_gate = t1_v
        v_scan_prev = None
        for h in range(H):
            if h < 4:
                slab_jobs = [(lambda k=k: emit_wmax_slab(32 + 8 * h + k))
                             for k in range(8)]
                slab_jobs += [(lambda k=k: emit_wmax_slab(4 * h + k))
                              for k in range(4)]
            else:
                slab_jobs = [(lambda k=k: emit_wmax_slab(16 + 4 * (h - 4) + k))
                             for k in range(4)]
                slab_jobs += [(lambda k=k: emit_p55_slab(8 * (h - 4) + k))
                              for k in range(8)]

            def do_slabs(n):
                for _ in range(n):
                    if slab_jobs:
                        slab_jobs.pop(0)()
            for (dst, src) in ((wqs, wq_h[h * 128:(h + 1) * 128, :]),
                               (wks, wk_h[h * 128:(h + 1) * 128, :]),
                               (wvgs, wvg_h[h * 128:(h + 1) * 128, :])):
                P.op("gpsimd", lambda e, dst=dst, src=src: e.dma_start(
                    out=dst[:], in_=src),
                    waits=[(sPE, head_pe.get(h - 1))],
                    inc=qG, inc_k=16)
            qg_w = P.op("gpsimd", lambda e, h=h: e.dma_start(
                out=wgk2s[:], in_=wgk2_h[:, h * 128:(h + 1) * 128]),
                inc=qG, inc_k=16)
            if h == 0:
                emit_xmax_launch()

            for ts4 in range(4):
                seq, buf, qs_x = load_xsl(ts4)

                def mmq(e, buf=buf):
                    for kb in range(16):
                        last = e.matmul(PB[0][:, 0:512],
                                        wqs[:, kb * 128:(kb + 1) * 128],
                                        buf[:, kb * 512:(kb + 1) * 512],
                                        start=(kb == 0), stop=(kb == 15))
                    return last
                pe_q = P.op("tensor", mmq,
                            waits=[qs_x, (qG, qg_w), bw("P0"),
                                   (sV, head_gate)], inc=sPE)
                ev_q = P.op("vector", lambda e, ts4=ts4: e.tensor_copy(
                    qT[:, ts4 * 512:(ts4 + 1) * 512], PB[0][:, 0:512]),
                    waits=[(sPE, pe_q)], inc=sV)
                bankw["P0"] = (sV, ev_q)

                def mmk(e, buf=buf):
                    for kb in range(16):
                        last = e.matmul(PB[1][:, 0:512],
                                        wks[:, kb * 128:(kb + 1) * 128],
                                        buf[:, kb * 512:(kb + 1) * 512],
                                        start=(kb == 0), stop=(kb == 15))
                    return last
                pe_k = P.op("tensor", mmk, waits=[bw("P1")], inc=sPE)
                ev_k = P.op("vector", lambda e, ts4=ts4: e.tensor_copy(
                    kT[:, ts4 * 512:(ts4 + 1) * 512], PB[1][:, 0:512]),
                    waits=[(sPE, pe_k)], inc=sV)
                bankw["P1"] = (sV, ev_k)

                for st in range(4):
                    pbk = 2 + (st % 2)

                    def mmvg(e, buf=buf, st=st, pbk=pbk):
                        for kb in range(16):
                            last = e.matmul(
                                PB[pbk][:, 0:512],
                                buf[:, kb * 512 + st * 128: kb * 512 + (st + 1) * 128],
                                wvgs[:, kb * 512:(kb + 1) * 512],
                                start=(kb == 0), stop=(kb == 15))
                        return last
                    pe_vg = P.op("tensor", mmvg, waits=[bw(f"P{pbk}")], inc=sPE)
                    ev_vg = P.op("vector", lambda e, ts4=ts4, st=st, pbk=pbk:
                                 e.tensor_copy(
                        vg[:, (ts4 * 4 + st) * 512:(ts4 * 4 + st + 1) * 512],
                        PB[pbk][:, 0:512]), waits=[(sPE, pe_vg)], inc=sV)
                    bankw[f"P{pbk}"] = (sV, ev_vg)

                pe_g = P.op("tensor", lambda e, ts4=ts4: e.matmul(
                    PB[0][:, 0:512], wgk2s[:, :], t1[:, ts4 * 512:(ts4 + 1) * 512],
                    start=True, stop=True), waits=[bw("P0")], inc=sPE)
                mark_xsl(seq, sPE, pe_g)
                ev_g = P.op("vector", lambda e, ts4=ts4, h=h: e.tensor_scalar_add(
                    scrA[:, ts4 * 512:(ts4 + 1) * 512], PB[0][:, 0:512],
                    bgcol[:, h:h + 1]),
                    waits=[(sPE, pe_g), (sA, a_abs.get(h - 1))], inc=sV)
                bankw["P0"] = (sV, ev_g)
                do_slabs(3)
            v_proj = sV.n

            a_abs[h] = P.op("scalar", lambda e: e.activation(
                scrB[:], scrA[:], AF.Abs), waits=[(sV, v_proj)], inc=sA)
            P.op("scalar", lambda e: e.activation(
                scrB[:], scrB[:], AF.Exp, scale=consts[:, C_M1:C_M1 + 1]))
            a_ls = P.op("scalar", lambda e: e.activation(
                scrB[:], scrB[:], AF.Ln, bias=1.0), inc=sA)
            P.op("vector", lambda e: e.tensor_scalar(
                scrA[:], scrA[:], 0.0, 1.0 / 16.0, ALU.min, ALU.mult),
                waits=[(sA, a_ls)])
            v_gk = P.op("vector", lambda e: e.scalar_tensor_tensor(
                gkT[:], scrB[:], -1.0 / 16.0, scrA[:], ALU.mult, ALU.add), inc=sV)
            v_bc = P.op("vector", lambda e: e.tensor_tensor_scan(
                scrB[:], cmask[:], gkT[:], 0.0, ALU.mult, ALU.add),
                waits=[(sV, v_gk)], inc=sV)

            a_e2 = P.op("scalar", lambda e: e.activation(
                e2b[:], scrB[:], AF.Exp), waits=[(sV, v_bc)], inc=sA)
            a_eb = P.op("scalar", lambda e: e.activation(
                eball[:], Bc3[:, :, CH - 1], AF.Exp), inc=sA)

            v_cp = P.op("vector", lambda e: e.tensor_tensor_scan(
                cpall[:], eball[:], onesc[:], 1.0, ALU.mult, ALU.mult),
                waits=[(sA, a_eb), (sV, v_ones)], inc=sV)
            v_kap = P.op("vector", lambda e: e.tensor_mul(
                kapall[:], eball[:], cpall[:]), waits=[(sV, v_cp)], inc=sV)

            v_sq = P.op("vector", lambda e: e.tensor_mul(
                sqt[:], qT[:], e2b[:]), waits=[(sA, a_e2)], inc=sV)
            a_e3 = P.op("scalar", lambda e: e.activation(
                e2b[:], scrB[:], AF.Exp,
                scale=consts[:, C_M1:C_M1 + 1]), waits=[(sV, v_sq)], inc=sA)
            v_sk = P.op("vector", lambda e: e.tensor_mul(
                skt[:], kT[:], e2b[:]), waits=[(sA, a_e3)], inc=sV)

            ilast = None
            for i in range(NCH):
                ilast = P.op("vector", lambda e, i=i: e.tensor_scalar_mul(
                    qkl[:, i * CH:(i + 1) * CH], qT[:, i * CH:(i + 1) * CH],
                    kapall[:, i:i + 1]),
                    waits=[(sV, v_kap), (qSPq, spill_qk.get(h - 1))] if i == 0 else [],
                    inc=sV if i == NCH - 1 else None)
            v_qkl = ilast

            for i in range(NCH):
                key = f"P{i % 2}"
                pe_p = P.op("tensor", lambda e, i=i: e.matmul(
                    PB[i % 2][0:64, 0:64], skt[:, i * CH:(i + 1) * CH],
                    sqt[:, i * CH:(i + 1) * CH], start=True, stop=True),
                    waits=[(sV, v_sq if i == 0 else None),
                           (sV, v_sk if i == 0 else None), bw(key)], inc=sPE)
                v_pe = P.op("vector", lambda e, i=i: e.tensor_mul(
                    pse[0:64, (i % 2) * 64:(i % 2 + 1) * 64],
                    PB[i % 2][0:64, 0:64], m64[:]), waits=[(sPE, pe_p)], inc=sV)
                bankw[key] = (sV, v_pe)
                P.op("vector", lambda e, i=i: e.tensor_reduce(
                    cscol[64 * (i % 2):64 * (i % 2) + 64, i // 2:i // 2 + 1],
                    pse[0:64, (i % 2) * 64:(i % 2 + 1) * 64], AXX, ALU.add),
                    waits=[(sV, v_pe)], inc=sV if i == NCH - 1 else None)
            v_cs = sV.n
            do_slabs(2)

            for tt in range(16):
                ilast = P.op("vector", lambda e, tt=tt: e.scalar_tensor_tensor(
                    ogasm[:, tt * DV:(tt + 1) * DV],
                    vg[:, tt * 512:tt * 512 + DV], cscol[:, tt:tt + 1],
                    vg[:, tt * 512 + DV:(tt + 1) * 512], ALU.mult, ALU.add),
                    waits=([(sV, v_cs), (qSPo, spill_og.get(h - 1))]
                           if tt == 0 else []),
                    inc=sV if tt == 15 else None)
            v_og0 = ilast

            for i in range(NCH):
                key = f"Pb_{i % 2}"
                pe_t = P.op("tensor", lambda e, i=i: e.transpose(
                    PBb[i % 2][0:64, 0:128], skt[:, i * CH:(i + 1) * CH],
                    identb[:]),
                    waits=[(sV, v_sk if i == 0 else None), bw(key)], inc=sPE)
                bankw[key] = (sV, P.op("vector", lambda e, i=i: e.tensor_copy(
                    k2a[64 * (i % 2):64 * (i % 2) + 64,
                        (i // 2) * 128:(i // 2 + 1) * 128],
                    PBb[i % 2][0:64, 0:128]), waits=[(sPE, pe_t)], inc=sV))
            v_k2 = sV.n
            do_slabs(len(slab_jobs))

            v_sfh = None
            for eh in range(2):
                P.op("vector", lambda e: e.memset(ebz3[:, :, 0], 0.0),
                     waits=[(sV, v_scan_prev)])
                ilast = None
                for i in range(1, NCH):
                    ilast = P.op("vector", lambda e, i=i: e.tensor_scalar_mul(
                        ebz3[:, :, i], ones256[:, 0:128], eball[:, i:i + 1]),
                        waits=[(sA, a_eb)] if i == 1 else [],
                        inc=sV if i == NCH - 1 else None)
                v_ebz = ilast

                for i in range(NCH):
                    key = f"P{2 + i % 2}"
                    pe_d = P.op("tensor", lambda e, i=i: e.matmul(
                        PB[2 + i % 2][:, 0:DV],
                        k2a[64 * (i % 2):64 * (i % 2) + 64,
                            (i // 2) * 128:(i // 2 + 1) * 128],
                        vg[64 * (i % 2):64 * (i % 2) + 64,
                           (i // 2) * 512:(i // 2) * 512 + DV],
                        start=True, stop=True),
                        waits=[(sV, v_k2 if (i == 0 and eh == 0) else None),
                               bw(key)], inc=sPE)
                    bankw[key] = (sV, P.op(
                        "vector", lambda e, i=i, eh=eh: e.tensor_scalar_mul(
                            dall3[:, :, i],
                            PB[2 + i % 2][:, eh * 128:(eh + 1) * 128],
                            eball[:, i:i + 1]),
                        waits=[(sPE, pe_d)], inc=sV))
                v_dall = sV.n

                v_scan = P.op("vector", lambda e: e.tensor_tensor_scan(
                    sall[:], ebz[:], dall[:], 0.0, ALU.mult, ALU.add),
                    waits=[(sV, v_ebz), (sV, v_dall)], inc=sV)
                v_scan_prev = v_scan

                o_pe, sc_v = {}, {}
                for i in range(NCH):
                    sc_v[i] = P.op("vector", lambda e, i=i, eh=eh: e.tensor_copy(
                        scp[:, (i % 2) * DV + eh * 128:
                            (i % 2) * DV + (eh + 1) * 128], sall3[:, :, i]),
                        waits=([(sV, v_scan)] if i == 0 else []) +
                        [(sPE, o_pe.get(i - 2))], inc=sV)
                    v_qt = P.op("vector", lambda e, i=i: e.tensor_scalar_mul(
                        qtp[:, (i % 2) * CH:(i % 2 + 1) * CH],
                        qT[:, i * CH:(i + 1) * CH], eball[:, i:i + 1]),
                        waits=[(sPE, o_pe.get(i - 2))], inc=sV)
                    key = f"P{i % 2}"
                    o_pe[i] = P.op("tensor", lambda e, i=i, eh=eh: e.matmul(
                        PB[i % 2][0:64, eh * 128:(eh + 1) * 128],
                        qtp[:, (i % 2) * CH:(i % 2 + 1) * CH],
                        scp[:, (i % 2) * DV + eh * 128:
                            (i % 2) * DV + (eh + 1) * 128],
                        start=True, stop=True),
                        waits=[(sV, sc_v[i]), (sV, v_qt), bw(key)], inc=sPE)
                    bankw[key] = (sV, P.op(
                        "vector", lambda e, i=i, eh=eh: e.tensor_add(
                            ogasm[64 * (i % 2):64 * (i % 2) + 64,
                                  (i // 2) * DV + eh * 128:
                                  (i // 2) * DV + (eh + 1) * 128],
                            PB[i % 2][0:64, eh * 128:(eh + 1) * 128],
                            ogasm[64 * (i % 2):64 * (i % 2) + 64,
                                  (i // 2) * DV + eh * 128:
                                  (i // 2) * DV + (eh + 1) * 128]),
                        waits=[(sPE, o_pe[i]),
                               (sV, v_og0 if (i == 0 and eh == 0) else None)],
                        inc=sV))
                v_sfh = P.op("vector", lambda e, eh=eh: e.tensor_copy(
                    sfin[:, eh * 128:(eh + 1) * 128],
                    scp[:, ((NCH - 1) % 2) * DV + eh * 128:
                        ((NCH - 1) % 2) * DV + (eh + 1) * 128]),
                    waits=[(sV, sc_v[NCH - 1]),
                           (qSPs, spill_sf.get(h - 1) if eh == 0 else None)],
                    inc=sV)
            v_ogdone = sV.n
            v_sf = v_sfh

            spill_og[h] = P.op("scalar", lambda e, h=h: e.dma_start(
                out=d_og_r[h], in_=ogasm[:]), waits=[(sV, v_ogdone)],
                inc=qSPo, inc_k=16)
            spill_qk[h] = P.op("scalar", lambda e, h=h: e.dma_start(
                out=d_qk_r[h], in_=qkl[:]), waits=[(sV, v_qkl)],
                inc=qSPq, inc_k=16)
            spill_sf[h] = P.op("scalar", lambda e, h=h: e.dma_start(
                out=cc2_src[:, h * DV:(h + 1) * DV], in_=sfin[:]),
                waits=[(sV, v_sf)], inc=qSPs, inc_k=16)

            head_pe[h] = sPE.n
            head_gate = sV.n
            if h == 0:
                emit_xmax_tail()
            if h == 3:
                emit_wscale_tail("wpr")
            if h == 7:
                emit_wscale_tail("wfc")
        v_rqx = xch["v_rqx"]
        v_sprb = wm_state["vals"][SC_SPR]
        v_c1 = wm_state["vals"]["v_c1"]
        v_c1b = wm_state["vals"][SC_C1]
        p15_v = wm_state["vals"][SC_RQFC]
        p55_v = wm_state["p55_v"]
        w8p_qc = qW8.n
        p3_qc = qC.n
        p3_pe = sPE.n
        p3_v = sV.n

        esB.close()

        cc2_n = P.op("gpsimd", lambda e: e.collective_compute(
            "AllGather", ALU.bypass, replica_groups=[list(range(NCORE))],
            ins=[cc2_src.opt()], outs=[cc2_out.opt()]),
            waits=[(qSPs, qSPs.n)], inc=sCC)
        cc2_blk = cc2_out.rearrange("(r p) c -> r p c", p=128)

        esP7 = esW.enter_context(ExitStack())
        sbP7 = mk_sb(esP7)
        spf32 = sbP7("spf32", [128, H * DV], F32)
        g_r = sbP7("g_r", [128, H * DV], F32)
        p7_state = {"v_sp": None, "v_spb": None}

        def emit_p7_step(r):
            v_sp = p7_state["v_sp"]
            qg_g = P.op("gpsimd", lambda e, r=r: e.dma_start(
                out=g_r[:], in_=cc2_blk[r, :, 0:2048]),
                waits=[(sCC, cc2_n), (sV, v_sp)], inc=qG, inc_k=16)
            if r == 0:
                v_sp = P.op("vector", lambda e: e.tensor_scalar_mul(
                    spf32[:], g_r[:], smask[:, 0:1]),
                    waits=[(qG, qg_g)], inc=sV)
            else:
                v_sp = P.op("vector", lambda e, r=r: e.scalar_tensor_tensor(
                    spf32[:], g_r[:], smask[:, r:r + 1], spf32[:],
                    ALU.mult, ALU.add), waits=[(qG, qg_g), (sV, v_sp)], inc=sV)
            p7_state["v_sp"] = v_sp

        esC = esW.enter_context(ExitStack())
        sbC = mk_sb(esC)
        x8 = sbC("x8", [128, 16 * T], F8)
        w8f = [sbC("w8f0", [128, 16 * 256], F8), sbC("w8f1", [128, 16 * 256], F8)]
        rbuf = sbC("rbuf", [128, 2 * 512], BF16)

        x83 = x8[:].rearrange("p (kb t) -> p kb t", kb=16)
        v_x8 = None
        for ts4 in range(4):
            seq, buf, qs_x = load_xsl(ts4)
            v_x8 = P.op("vector", lambda e, ts4=ts4, buf=buf: e.tensor_scalar_mul(
                x83[:, :, ts4 * 512:(ts4 + 1) * 512],
                buf[:].rearrange("p (kb st q) -> p kb (st q)", kb=16, st=4),
                scal[:, SC_RQX:SC_RQX + 1]),
                waits=[qs_x, (sV, v_rqx),
                       (sV, p3_v if ts4 == 0 else None)],
                inc=sV)
            mark_xsl(seq, sV, v_x8)
        v_x8f = v_x8

        d_h_bl = d_h.rearrange("(jt tt p) q -> jt tt p q", tt=4, p=128)
        fc1_vq, fc1_pe = {}, {}
        rb_guard = {}
        for jsl in range(32):
            if 8 <= jsl < 16:
                emit_p7_step(jsl - 8)
            elif jsl == 16:
                p7_state["qg_sp"] = P.op(
                    "gpsimd", lambda e: e.dma_start(out=d_sp[:], in_=spf32[:]),
                    waits=[(sV, p7_state["v_sp"])], inc=qSP2, inc_k=16)
            buf = wsl[jsl % 2]
            qs_w = P.op("sync", lambda e, jsl=jsl, buf=buf: e.dma_start(
                out=buf[:].rearrange("p (a j) -> p a j", a=16),
                in_=wfcT_r[:, :, jsl * 256:(jsl + 1) * 256]),
                waits=[(sV, p55_v if jsl < 2 else fc1_vq.get(jsl - 2))],
                inc=qWL[jsl % 2], inc_k=16)
            w8fb = w8f[jsl % 2]
            fc1_vq[jsl] = P.op("vector", lambda e, buf=buf, w8fb=w8fb:
                               e.tensor_scalar_mul(
                w8fb[:], buf[:], scal[:, SC_RQFC:SC_RQFC + 1]),
                waits=[(qWL[jsl % 2], qs_w), (sV, p15_v), (sPE, fc1_pe.get(jsl - 2))],
                inc=sV)
            w8fb3 = w8fb[:].rearrange("p (a j) -> p a j", a=16)
            for jj in range(2):
                def mmfc(e, jj=jj, w8fb3=w8fb3):
                    for kt2 in range(8):
                        for tt in range(4):
                            last = e.matmul(
                                PB[tt][:, 0:512],
                                w8fb3[:, 2 * kt2:2 * kt2 + 2,
                                      jj * 128:(jj + 1) * 128],
                                x83[:, 2 * kt2:2 * kt2 + 2,
                                    tt * 512:(tt + 1) * 512],
                                start=(kt2 == 0), stop=(kt2 == 7),
                                perf_mode=mybir.MatmulPerfMode.DoubleRow)
                    return last
                pe_f = P.op("tensor", mmfc,
                            waits=[(sV, fc1_vq[jsl]), (sV, v_x8f),
                                   bw("P0"), bw("P1"), bw("P2"), bw("P3")],
                            inc=sPE)
                fc1_pe[jsl] = pe_f
                for tt in range(4):
                    par = tt % 2
                    gqc, gsv = rb_guard.get(par, (None, None))
                    a_r = P.op("scalar", lambda e, tt=tt, par=par: e.activation(
                        rbuf[:, par * 512:(par + 1) * 512], PB[tt][:, 0:512],
                        AF.Relu),
                        waits=[(sPE, pe_f), (qDH[par], gqc), (sV, gsv)], inc=sA)
                    bankw[f"P{tt}"] = (sA, a_r)
                    v_r4 = P.op("vector", lambda e, par=par: e.tensor_reduce(
                        red_t[:, 1:2], rbuf[:, par * 512:(par + 1) * 512],
                        AXX, ALU.max), waits=[(sA, a_r)], inc=sV)
                    g_m = P.op("vector", lambda e: e.tensor_max(
                        xmax_c[:, 3:4], xmax_c[:, 3:4], red_t[:, 1:2]),
                        waits=[(sV, v_r4)], inc=sV)
                    q_s = P.op("scalar", lambda e, jsl=jsl, jj=jj, tt=tt, par=par:
                               e.dma_start(out=d_h_bl[jsl * 2 + jj, tt],
                                           in_=rbuf[:, par * 512:(par + 1) * 512]),
                               inc=qDH[par], inc_k=16)
                    rb_guard[par] = (q_s, g_m)
        fc1_dh = (qDH[0].n, qDH[1].n)
        fc1_g = sV.n
        fc1_pe_fin = sPE.n

        pe_rm = P.op("tensor", lambda e: e.transpose(
            PB2[0][0:1, 0:128], xmax_c[:, 3:4], identf[:]),
            waits=[(sV, fc1_g), bw("P2_0")], inc=sPE)
        v_rm = P.op("vector", lambda e: e.tensor_reduce(
            one1[0:1, O_RM:O_RM + 1], PB2[0][0:1, 0:128], AXX, ALU.max),
            waits=[(sPE, pe_rm)], inc=sV)
        bankw["P2_0"] = (sV, v_rm)
        v_h2a = P.op("vector", lambda e: e.tensor_mul(
            one1[0:1, O_H2L:O_H2L + 1], one1[0:1, O_RM:O_RM + 1],
            one1[0:1, O_C1:O_C1 + 1]), waits=[(sV, v_c1), (sV, v_rm)], inc=sV)
        v_h2l = P.op("vector", lambda e: e.tensor_scalar_mul(
            one1[0:1, O_H2L:O_H2L + 1], one1[0:1, O_H2L:O_H2L + 1],
            one1[0:1, O_H2L:O_H2L + 1]), waits=[(sV, v_h2a)], inc=sV)
        v_h2row = P.op("vector", lambda e: e.tensor_scalar_mul(
            cc1row[0:1, :], onesrow[0:1, :], one1[0:1, O_H2L:O_H2L + 1]),
            waits=[(sV, v_h2l)], inc=sV)
        qg_spst = p7_state["qg_sp"]

        def emit_sh_chain():
            shcol = scal[:, SC_SH:SC_SH + 1]
            v_shs = P.op("vector", lambda e: e.tensor_scalar(
                one1[0:1, O_T2:O_T2 + 1], one1[0:1, O_T2:O_T2 + 1],
                1.0 / 448.0, 1e-6, ALU.mult, ALU.max),
                waits=[(qC1, qg_h2o)], inc=sV)
            v_sh = bcast_col(O_T2, SC_SH, v_shs)
            v_sh2 = P.op("vector", lambda e: e.tensor_scalar_mul(
                h2col[:, 0:1], shcol, 2.0), waits=[(sV, v_sh)], inc=sV)
            v_rc7 = P.op("vector", lambda e: e.reciprocal(
                h2col[:, 1:2], h2col[:, 0:1]), waits=[(sV, v_sh2)], inc=sV)
            v_c1s = P.op("vector", lambda e: e.tensor_scalar_mul(
                h2col[:, 2:3], scal[:, SC_C1:SC_C1 + 1],
                scal[:, SC_C1:SC_C1 + 1]),
                waits=[(sV, v_c1b)], inc=sV)
            v_f = P.op("vector", lambda e: e.tensor_mul(
                h2col[:, 2:3], h2col[:, 2:3], h2col[:, 1:2]),
                waits=[(sV, v_rc7), (sV, v_c1s)], inc=sV)
            P.op("scalar", lambda e: e.activation(
                h2col[:, 3:4], h2col[:, 2:3], AF.Ln), waits=[(sV, v_f)])
            a_sqf = P.op("scalar", lambda e: e.activation(
                scal[:, SC_SQF:SC_SQF + 1], h2col[:, 3:4], AF.Exp,
                scale=consts[:, C_P05:C_P05 + 1]), inc=sA)
            v_sh4 = P.op("vector", lambda e: e.tensor_scalar_mul(
                h2col[:, 4:5], shcol, 4.0),
                waits=[(sA, a_sqf), (sV, v_sprb)], inc=sV)
            v_c2 = P.op("vector", lambda e: e.tensor_mul(
                scal[:, SC_C2:SC_C2 + 1], h2col[:, 4:5],
                scal[:, SC_SPR:SC_SPR + 1]), waits=[(sV, v_sh4)], inc=sV)
            return a_sqf, v_c2

        esC.close()
        esP7.close()
        esW.close()

        esO = es.enter_context(ExitStack())
        sbO = mk_sb(esO)
        h8ab = [sbO("h8a", [128, 64 * 512], F8), sbO("h8b", [128, 64 * 512], F8)]
        hstg = sbO("hstg", [128, 6 * 512], BF16)
        sbst = sbO("sbst", [128, 2 * 512], BF16)

        esD = esO.enter_context(ExitStack())
        sbD = mk_sb(esD)
        sprev = sbD("sprev", [128, H * DV], BF16)
        woin = sbD("woin", [128, 16 * T], BF16)
        qg_spld = P.op("gpsimd", lambda e: e.dma_start(
            out=sprev[:], in_=d_sp[:]),
            waits=[(qSP2, qg_spst), (sPE, fc1_pe_fin), (sV, fc1_g),
                   (qDH[0], fc1_dh[0]), (qDH[1], fc1_dh[1])],
            inc=qSP2, inc_k=16)
        wo_mm_l = {}
        wo_ld = {}

        def emit_wosl_ld(ct):
            wp = ct % 2
            wo_ld[ct] = P.op("gpsimd", lambda e, ct=ct, wp=wp: e.dma_start(
                out=wosl2[wp][:], in_=wo_h[ct * 128:(ct + 1) * 128, :]),
                waits=[(sPE, wo_mm_l.get(ct - 2)
                        if ct >= 2 else fc1_pe_fin)],
                inc=qWO, inc_k=16)

        emit_wosl_ld(0)
        emit_wosl_ld(1)
        qg_h2s = P.op("gpsimd", lambda e: e.dma_start(
            out=cch_src[:], in_=cc1row[:]),
            waits=[(sV, v_h2row)], inc=qC1, inc_k=16)
        h2_n = P.op("gpsimd", lambda e: e.collective_compute(
            "AllReduce", ALU.max, replica_groups=[list(range(NCORE))],
            ins=[cch_src.opt()], outs=[cch_out.opt()]),
            waits=[(qC1, qg_h2s)], inc=sCC)
        qg_h2o = P.op("gpsimd", lambda e: e.dma_start(
            out=one1[0:1, O_T2:O_T2 + 1], in_=cch_out[0:1, 0:1]),
            waits=[(sCC, h2_n)], inc=qC1, inc_k=16)
        v_spb = None
        og16 = sbD("og16", [128, 16 * DV], BF16)
        qk_s = sbD("qk_s", [128, T], BF16)
        af = sbD("af", [128, DV], BF16)
        afs = sbD("afs", [128, DV], BF16)
        dmp = sbD("dmp", [128, DV], F32)
        wosl2 = [sbD("wosl0", [128, 16 * 512], BF16),
                 sbD("wosl1", [128, 16 * 512], BF16)]

        p8_v = None
        a_si_prev = None
        pe_tr_prev = None
        p8_vrb = {}
        a_sqf, v_c2 = None, None
        for h in range(H):
            if h == 1:
                a_sqf, v_c2 = emit_sh_chain()
            qc_og = P.op("scalar", lambda e, h=h: e.dma_start(
                out=og16[:], in_=d_og_r[h]),
                waits=[(qSPo, spill_og[h]), (sV, p8_v)] +
                ([(sV, fc1_g), (sPE, fc1_pe_fin),
                  (qDH[0], fc1_dh[0]), (qDH[1], fc1_dh[1])]
                 if h == 0 else []),
                inc=qRO, inc_k=16)
            qc_qk = P.op("scalar", lambda e, h=h: e.dma_start(
                out=qk_s[:], in_=d_qk_r[h]),
                waits=[(qSPq, spill_qk[h])], inc=qRQ, inc_k=16)
            for tt in range(16):
                key = f"P{tt % 2}"
                pe_c = P.op("tensor", lambda e, h=h, tt=tt: e.matmul(
                    PB[tt % 2][:, 0:DV], qk_s[:, tt * 128:(tt + 1) * 128],
                    sprev[:, h * DV:(h + 1) * DV], start=True, stop=True),
                    waits=[(qRQ, qc_qk), (qSP2, qg_spld), bw(key)],
                    inc=sPE)
                v_o = P.op("vector", lambda e, tt=tt: e.tensor_add(
                    og16[:, tt * DV:(tt + 1) * DV], PB[tt % 2][:, 0:DV],
                    og16[:, tt * DV:(tt + 1) * DV]),
                    waits=[(sPE, pe_c), (qRO, qc_og if tt == 0 else None)],
                    inc=sV)
                bankw[key] = (sV, v_o)
                a_q8 = P.op("scalar", lambda e, tt=tt: e.activation(
                    dmp[:], og16[:, tt * DV:(tt + 1) * DV], AF.Square),
                    waits=[(sV, v_o), (sV, p8_vrb.get((tt - 1) % 16))], inc=sA)
                p8_vrb[tt] = P.op("vector", lambda e, tt=tt: e.tensor_reduce(
                    stat_b[:, tt:tt + 1], dmp[:], AXX, ALU.add),
                    waits=[(sA, a_q8)], inc=sV)
                P.op("vector", lambda e, tt=tt: e.tensor_reduce(
                    stat_a[:, tt:tt + 1], og16[:, tt * DV:(tt + 1) * DV],
                    AXX, ALU.add), inc=sV if tt == 15 else None)
            v_st = sV.n
            a_st = sA.n
            v_8a = P.op("vector", lambda e: e.tensor_scalar_mul(
                stat_a[:, 0:16], stat_a[:, 0:16], 1.0 / DV),
                waits=[(sA, a_st), (sV, v_st)], inc=sV)
            v_8b = P.op("vector", lambda e: e.tensor_scalar_mul(
                stat_b[:, 0:16], stat_b[:, 0:16], 1.0 / DV),
                waits=[(sV, p8_vrb[15])], inc=sV)
            v_8c = P.op("vector", lambda e: e.tensor_mul(
                stat_c[:, 0:16], stat_a[:, 0:16],
                stat_a[:, 0:16] if False else stat_a[:, 0:16]),
                waits=[(sV, v_8a)], inc=sV)
            v_va = P.op("vector", lambda e: e.tensor_sub(
                stat_c[:, 0:16], stat_b[:, 0:16], stat_c[:, 0:16]),
                waits=[(sV, v_8b), (sV, v_8c)], inc=sV)
            P.op("scalar", lambda e: e.activation(
                stat_c[:, 0:16], stat_c[:, 0:16], AF.Ln,
                bias=consts[:, C_EPS:C_EPS + 1]), waits=[(sV, v_va)])
            a_rs = P.op("scalar", lambda e: e.activation(
                stat_c[:, 0:16], stat_c[:, 0:16], AF.Exp,
                scale=consts[:, C_M05:C_M05 + 1]), inc=sA)
            for tt in range(16):
                v_n = P.op("vector", lambda e, tt=tt: e.tensor_scalar(
                    af[:], og16[:, tt * DV:(tt + 1) * DV],
                    stat_a[:, tt:tt + 1], stat_c[:, tt:tt + 1],
                    ALU.subtract, ALU.mult),
                    waits=[(sA, a_rs), (sA, a_si_prev)], inc=sV)
                v_nw = P.op("vector", lambda e: e.tensor_mul(
                    af[:], af[:], lnwb[:]), waits=[(sV, v_n)], inc=sV)
                v_af = P.op("vector", lambda e: e.tensor_add(
                    af[:], af[:], lnbb[:]), waits=[(sV, v_nw)], inc=sV)
                a_si = P.op("scalar", lambda e: e.activation(
                    afs[:], af[:], AF.Silu),
                    waits=[(sV, v_af), (sPE, pe_tr_prev)], inc=sA)
                a_si_prev = a_si
                for eh in range(2):
                    key = f"Pb_{eh}"
                    pe_t = P.op("tensor", lambda e, eh=eh: e.transpose(
                        PBb[eh][0:128, 0:128], afs[:, eh * 128:(eh + 1) * 128],
                        identb[:]), waits=[(sA, a_si), bw(key)], inc=sPE)
                    pe_tr_prev = pe_t
                    bankw[key] = (sV, P.op(
                        "vector", lambda e, h=h, tt=tt, eh=eh: e.tensor_copy(
                            woin[:, (2 * h + eh) * T + tt * 128:
                                 (2 * h + eh) * T + (tt + 1) * 128],
                            PBb[eh][0:128, 0:128]), waits=[(sPE, pe_t)], inc=sV))
            p8_v = sV.n
        p8_done = p8_v

        pe_wo_last = None
        sb_st = {}
        sb_prev = {}
        for ct in range(4):
            qg_wo = wo_ld[ct]
            woslb = wosl2[ct % 2]
            for tt in range(16):
                key = f"P{tt % 2}"

                def mmwo(e, ct=ct, tt=tt, woslb=woslb):
                    for vd in range(16):
                        last = e.matmul(
                            PB[tt % 2][:, 0:512],
                            woin[:, vd * T + tt * 128:vd * T + (tt + 1) * 128],
                            woslb[:, vd * 512:(vd + 1) * 512],
                            start=(vd == 0), stop=(vd == 15))
                    return last
                pe_wo = P.op("tensor", mmwo,
                             waits=[(qWO, qg_wo), (sV, p8_done), bw(key)], inc=sPE)
                if tt == 15:
                    wo_mm_l[ct] = pe_wo
                    if ct + 2 < 4:
                        emit_wosl_ld(ct + 2)
                pe_wo_last = pe_wo
                par = (ct * 16 + tt) % 2
                ev = P.op("vector", lambda e, par=par, tt=tt: e.tensor_copy(
                    sbst[:, par * 512:(par + 1) * 512], PB[tt % 2][:, 0:512]),
                    waits=[(sPE, pe_wo), (qSB[par], sb_prev.get(par))], inc=sV)
                bankw[key] = (sV, ev)
                sb_st[(tt, ct)] = P.op("sync", lambda e, ct=ct, tt=tt, par=par:
                                       e.dma_start(
                    out=d_sbf[:, tt * T + ct * 512:tt * T + (ct + 1) * 512],
                    in_=sbst[:, par * 512:(par + 1) * 512]),
                    waits=[(sV, ev)], inc=qSB[par], inc_k=16)
                sb_prev[par] = sb_st[(tt, ct)]
        p9_v = sV.n
        pe_wo_fin = pe_wo_last

        esD.close()

        esE = esO.enter_context(ExitStack())
        sbE = mk_sb(esE)
        w8ps2 = [sbE("w8ps0", [128, 64 * 512], F8),
                 sbE("w8ps1", [128, 64 * 512], F8)]
        gf = sbE("gf", [128, 512], F32)
        xf = [sbE("xf0", [128, 512], F32), sbE("xf1", [128, 512], F32)]
        of = [sbE("of0", [128, 512], F32), sbE("of1", [128, 512], F32)]
        sbl = [sbE("sbl0", [128, 512], BF16), sbE("sbl1", [128, 512], BF16)]

        d_h_ld = d_h.rearrange("(jt tt p) q -> jt tt p q", tt=4, p=128)

        hstg_a = {}
        v_m_prev = None
        last_dma = {}
        last_vo = {}
        last_vm = {}
        tg_mm_last = {}
        w8_mm_last = {}
        h8_done = {}
        hld_seq = [0]
        pe_last = None

        hb_pend = []

        def emit_h8_load(tg, jt):
            par6 = hld_seq[0] % 6
            hld_seq[0] += 1
            first = (tg == 0 and jt == 0)
            qs_hld = P.op("scalar", lambda e, jt=jt, tg=tg, par6=par6:
                          e.dma_start(
                out=hstg[:, par6 * 512:(par6 + 1) * 512],
                in_=d_h_ld[jt, tg]),
                waits=[(qDH[0], fc1_dh[0]), (qDH[1], fc1_dh[1]),
                       (sA, hstg_a.get(par6))] +
                ([(sPE, fc1_pe_fin), (sV, fc1_g), (qSP2, qg_spst)]
                 if first else []),
                inc=qH6[par6], inc_k=16)
            hb_pend.append((tg, jt, par6, qs_hld, first))

        def emit_h8_square():
            tg, jt, par6, qs_hld, first = hb_pend.pop(0)
            h8b_ = h8ab[tg % 2]
            hstg_a[par6] = P.op("scalar", lambda e, jt=jt, par6=par6,
                                h8b_=h8b_: e.activation(
                h8b_[:, jt * 512:(jt + 1) * 512],
                hstg[:, par6 * 512:(par6 + 1) * 512],
                AF.Square, scale=scal[:, SC_SQF:SC_SQF + 1]),
                waits=[(qH6[par6], qs_hld),
                       (sA, a_sqf if first else None),
                       (sPE, tg_mm_last.get(tg - 2) if jt == 0 else None)],
                inc=sA)

        def emit_h8_build(tg, jlo, jhi):
            for jt in range(jlo, jhi):
                emit_h8_load(tg, jt)
                if len(hb_pend) > 5:
                    emit_h8_square()
            if jhi == 64:
                while hb_pend:
                    emit_h8_square()
                h8_done[tg] = sA.n

        emit_h8_build(0, 0, 64)
        for tg in range(4):
            h8b_ = h8ab[tg % 2]
            for ct in range(4):
                if tg < 3:
                    emit_h8_build(tg + 1, ct * 16, (ct + 1) * 16)
                wpar = (tg * 4 + ct) % 2
                w8psb = w8ps2[wpar]
                qg_w8 = P.op("gpsimd", lambda e, ct=ct, w8psb=w8psb:
                             e.dma_start(
                    out=w8psb[:], in_=d_w8p[ct * 128:(ct + 1) * 128, :]),
                    waits=[(qW8, w8p_qc if (tg == 0 and ct < 2) else 0),
                           (sPE, w8_mm_last.get(wpar)
                            if (tg > 0 or ct > 1) else pe_wo_fin)],
                    inc=qW2[wpar], inc_k=16)
                for tt in range(4):
                    tta = tg * 4 + tt
                    key = f"P{tt % 2}"

                    def mmf2(e, tt=tt, h8b_=h8b_, w8psb=w8psb):
                        h83 = h8b_[:].rearrange("p (j q) -> p j q", j=64)
                        w83 = w8psb[:].rearrange("p (j q) -> p j q", j=64)
                        for jt2 in range(32):
                            last = e.matmul(
                                PB[tt % 2][:, 0:512],
                                h83[:, 2 * jt2:2 * jt2 + 2,
                                    tt * 128:(tt + 1) * 128],
                                w83[:, 2 * jt2:2 * jt2 + 2, :],
                                start=(jt2 == 0), stop=(jt2 == 31),
                                perf_mode=mybir.MatmulPerfMode.DoubleRow)
                        return last
                    pe_f2 = P.op("tensor", mmf2,
                                 waits=[(qW2[wpar], qg_w8),
                                        (sA, h8_done[tg]), bw(key)],
                                 inc=sPE)
                    pe_last = pe_f2
                    if tt == 3:
                        w8_mm_last[wpar] = pe_f2
                        tg_mm_last[tg] = pe_f2
                    a_gf = P.op("scalar", lambda e, tt=tt: e.activation(
                        gf[:], PB[tt % 2][:, 0:512], AF.Silu,
                        scale=scal[:, SC_C2:SC_C2 + 1]),
                        waits=[(sPE, pe_f2), (sV, v_c2), (sV, v_m_prev)],
                        inc=sA)
                    bankw[key] = (sA, a_gf)
                    par = tt % 2
                    qs_xf = P.op("sync", lambda e, tta=tta, ct=ct, par=par:
                                 e.dma_start(
                        out=xf[par][:],
                        in_=x_h[tta * 128:(tta + 1) * 128,
                                ct * 512:(ct + 1) * 512]),
                        waits=[(sV, last_vo.get(par))] +
                        ([(sPE, pe_wo_fin)] if (tg == 0 and ct == 0 and tt < 2)
                         else []),
                        inc=qXF[par], inc_k=16)
                    qs_sb = P.op("sync", lambda e, tta=tta, ct=ct, par=par:
                                 e.dma_start(
                        out=sbl[par][:],
                        in_=d_sbf[:, tta * T + ct * 512:
                                  tta * T + (ct + 1) * 512]),
                        waits=[(qSB[(ct * 16 + tta) % 2], sb_st[(tta, ct)]),
                               (sV, last_vm.get(par))] +
                        ([(sPE, pe_wo_fin)] if (tg == 0 and ct == 0 and tt < 2)
                         else []),
                        inc=qSBL[par], inc_k=16)
                    v_m = P.op("vector", lambda e, par=par: e.tensor_mul(
                        of[par][:], sbl[par][:], gf[:]),
                        waits=[(sA, a_gf), (qXF[par], qs_xf),
                               (qSBL[par], qs_sb),
                               (qY[par], last_dma.get(par))], inc=sV)
                    v_m_prev = v_m
                    last_vm[par] = v_m
                    v_o = P.op("vector", lambda e, par=par: e.tensor_add(
                        of[par][:], of[par][:], xf[par][:]),
                        waits=[(sV, v_m)], inc=sV)
                    last_vo[par] = v_o
                    last_dma[par] = P.op("sync", lambda e, tta=tta, ct=ct,
                                         par=par: e.dma_start(
                        out=y_h[tta * 128:(tta + 1) * 128,
                                ct * 512:(ct + 1) * 512],
                        in_=of[par][:]), waits=[(sV, v_o)],
                        inc=qY[par], inc_k=16)
        if debug:
            P.op("gpsimd", lambda e: e.dma_start(out=t_one1[:], in_=one1[:]),
                 waits=[(sV, v_c2)], inc=qG, inc_k=16)
            P.op("gpsimd", lambda e: e.dma_start(out=t_scal[:], in_=scal[:]),
                 inc=qG, inc_k=16)
            P.op("gpsimd", lambda e: e.dma_start(out=t_xn[:], in_=d_xn[:, :]),
                 inc=qG, inc_k=16)
            P.op("gpsimd", lambda e: e.dma_start(out=t_og[:], in_=d_og[:, :]),
                 inc=qG, inc_k=16)
            P.op("gpsimd", lambda e: e.dma_start(out=t_qk[:], in_=d_qk[:, :]),
                 inc=qG, inc_k=16)
            P.op("gpsimd", lambda e: e.dma_start(out=t_h[:], in_=d_h[0:256, :]),
                 inc=qG, inc_k=16)
            P.op("gpsimd", lambda e: e.dma_start(out=t_cc2[:], in_=cc2_out[:, :]),
                 inc=qG, inc_k=16)
            P.op("gpsimd", lambda e: e.dma_start(out=t_sbf[:], in_=d_sbf[:]),
                 waits=[(qSB[0], qSB[0].n), (qSB[1], qSB[1].n)],
                 inc=qG, inc_k=16)
            P.op("gpsimd", lambda e: e.dma_start(out=t_sprev[:], in_=sprev[:]),
                 waits=[(sV, v_spb)], inc=qG, inc_k=16)
            P.op("gpsimd", lambda e: e.dma_start(out=t_sdump[:], in_=sdump[:]),
                 inc=qG, inc_k=16)
        esE.close()
        esO.close()

        fin_waits = [(c, c.n) for c in sems
                     if c not in (sPE, sV, sA, sG, sCC) and c.n > 0]
        P.op("sync", lambda e: None, waits=fin_waits)

        ser_sems = {}
        for en in ser_engines:
            ser_sems[en] = es.enter_context(nc.semaphore(f"ser_{en}"))
        blk = es.enter_context(nc.Block())
        blk.sync(lambda e: P.replay("sync", e, ser_sems.get("sync")))
        blk.scalar(lambda e: P.replay("scalar", e, ser_sems.get("scalar")))
        blk.vector(lambda e: P.replay("vector", e, ser_sems.get("vector")))
        blk.tensor(lambda e: P.replay("tensor", e, ser_sems.get("tensor")))
        blk.gpsimd(lambda e: P.replay("gpsimd", e, ser_sems.get("gpsimd")))

    return nc


_NC_CACHE = {}


def kernel(L0_attn_hidden_states, L0_attn_Wq, L0_attn_Wk, L0_attn_Wv,
           L0_attn_Wg, L0_attn_Wgk1, L0_attn_Wgk2, L0_attn_bgk2, L0_attn_Wo,
           L0_attn_ln_w, L0_attn_ln_b, L0_ffn_w_fc, L0_ffn_w_proj):
    x = np.ascontiguousarray(np.asarray(L0_attn_hidden_states, np.float32))
    B, S, _ = x.shape
    wq0 = np.asarray(L0_attn_Wq, np.float32)
    wk0 = np.asarray(L0_attn_Wk, np.float32)
    wv = np.asarray(L0_attn_Wv, np.float32)
    wg = np.asarray(L0_attn_Wg, np.float32)
    wq = np.ascontiguousarray(
        wq0.reshape(16, 128, 8, 128).transpose(2, 1, 0, 3).reshape(1024, 2048))
    wk = np.ascontiguousarray(
        wk0.reshape(16, 128, 8, 128).transpose(2, 1, 0, 3).reshape(1024, 2048))
    wvg0 = np.concatenate(
        [np.concatenate([wv[:, h * DV:(h + 1) * DV],
                         wg[:, h * DV:(h + 1) * DV]], axis=1)
         for h in range(H)], axis=1)
    wvg = np.ascontiguousarray(
        wvg0.reshape(16, 128, 8, 512).transpose(2, 1, 0, 3).reshape(1024, 8192))
    wgk1 = np.ascontiguousarray(np.asarray(L0_attn_Wgk1, np.float32))
    wgk2 = np.ascontiguousarray(np.asarray(L0_attn_Wgk2, np.float32))
    bg = np.ascontiguousarray(
        np.asarray(L0_attn_bgk2, np.float32).reshape(H, 128).T)
    wo = np.ascontiguousarray(
        np.asarray(L0_attn_Wo, np.float32).reshape(16, 128, 4, 512)
        .transpose(2, 1, 0, 3).reshape(512, 8192))
    lnw = np.ascontiguousarray(np.asarray(L0_attn_ln_w, np.float32)[None, :])
    lnb = np.ascontiguousarray(np.asarray(L0_attn_ln_b, np.float32)[None, :])
    wfcT = np.ascontiguousarray(np.asarray(L0_ffn_w_fc, np.float32).T)
    wprT = np.ascontiguousarray(np.asarray(L0_ffn_w_proj, np.float32).T)

    identb = np.eye(128, dtype=ml_dtypes.bfloat16)
    identf = np.eye(128, dtype=np.float32)
    n_i = np.arange(64)
    m64 = ((n_i[None, :] >= n_i[:, None]) &
           (n_i[None, :] // 16 == n_i[:, None] // 16)).astype(np.float32)
    cm = np.ones((128, T), np.float32)
    cm[:, ::CH] = 0.0
    cmask = cm.astype(ml_dtypes.bfloat16)
    consts = np.zeros((128, 8), np.float32)
    consts[:, C_EPS] = EPS
    consts[:, C_M05] = -0.5
    consts[:, C_M1] = -1.0
    consts[:, C_P05] = 0.5
    onesrow = np.ones((1, 128), np.float32)

    if "prog" not in _NC_CACHE:
        _NC_CACHE["prog"] = build_program()
    nc = _NC_CACHE["prog"]

    in_maps = []
    for c in range(NCORE):
        b, s = c // 2, c % 2
        smask = np.zeros((128, 8), np.float32)
        if s == 1:
            smask[:, c - 1] = 1.0
        in_maps.append(dict(
            x_own=np.ascontiguousarray(x[b, s * T:(s + 1) * T, :]),
            Wq=wq, Wk=wk, Wvg=wvg, Wgk1=wgk1, Wgk2=wgk2, bgk2c=bg, Wo=wo,
            ln_w=lnw, ln_b=lnb, wfcT=wfcT, wprT=wprT,
            identb=identb, identf=identf, m64=m64, cmask=cmask,
            consts=consts, onesrow=onesrow, smask=smask))
    res = run_bass_kernel_spmd(nc, in_maps, core_ids=list(range(NCORE)))
    out = np.empty((B, S, D), np.float32)
    for c in range(NCORE):
        b, s = c // 2, c % 2
        out[b, s * T:(s + 1) * T, :] = res.results[c]["y"]
    return out


if __name__ == "__main__":
    build_program()
    print("build ok")


# version 17
# speedup vs baseline: 1.0129x; 1.0129x over previous
import numpy as np
import ml_dtypes
import concourse.bass as bass
import concourse.mybir as mybir
from concourse.bass_utils import run_bass_kernel_spmd

F32 = mybir.dt.float32
BF16 = mybir.dt.bfloat16
F8 = mybir.dt.float8e4
AF = mybir.ActivationFunctionType
ALU = mybir.AluOpType
AXX = mybir.AxisListType.X

NCORE, T, D, H, DK, DV, CH, FI = 8, 2048, 2048, 8, 128, 256, 64, 8192
NCH = T // CH
EPS = 1e-5
C_EPS, C_M05, C_M1, C_P05 = 0, 1, 2, 3

ENGS = ("sync", "scalar", "vector", "tensor", "gpsimd")


class Ctr:
    def __init__(self, handle, name):
        self.h, self.name, self.n = handle, name, 0


class Prog:
    def __init__(self):
        self.q = {e: [] for e in ENGS}

    def op(self, eng, builder, waits=(), inc=None, inc_k=1):
        w = [(c, v) for (c, v) in waits
             if c is not None and v is not None and v > 0]
        if inc is not None:
            inc.n += inc_k
        self.q[eng].append((w, builder, inc, inc_k, inc.n if inc else None))
        return inc.n if inc is not None else None

    def replay(self, eng_name, eng, ser=None):
        n = 0
        for w, builder, inc, k, post in self.q[eng_name]:
            for c, v in w:
                eng.wait_ge(c.h, v)
            inst = builder(eng)
            if inc is not None:
                inst.then_inc(inc.h, k)
            if ser is not None and inst is not None:
                if inc is not None:
                    eng.wait_ge(inc.h, post)
                else:
                    n += 1
                    inst.then_inc(ser, 1)
                    eng.wait_ge(ser, n)


def build_program(debug=False, ser_engines=()):
    nc = bass.Bass("TRN2", target_bir_lowering=False, debug=False)

    x_h = nc.dram_tensor("x_own", [T, D], F32, kind="ExternalInput").ap()
    wq_h = nc.dram_tensor("Wq", [H * 128, 16 * 128], F32, kind="ExternalInput").ap()
    wk_h = nc.dram_tensor("Wk", [H * 128, 16 * 128], F32, kind="ExternalInput").ap()
    wvg_h = nc.dram_tensor("Wvg", [H * 128, 16 * 512], F32, kind="ExternalInput").ap()
    wgk1_h = nc.dram_tensor("Wgk1", [D, 16], F32, kind="ExternalInput").ap()
    wgk2_h = nc.dram_tensor("Wgk2", [16, 1024], F32, kind="ExternalInput").ap()
    bgk2_h = nc.dram_tensor("bgk2c", [128, H], F32, kind="ExternalInput").ap()
    wo_h = nc.dram_tensor("Wo", [4 * 128, 16 * 512], F32, kind="ExternalInput").ap()
    lnw_h = nc.dram_tensor("ln_w", [1, DV], F32, kind="ExternalInput").ap()
    lnb_h = nc.dram_tensor("ln_b", [1, DV], F32, kind="ExternalInput").ap()
    wfcT_h = nc.dram_tensor("wfcT", [D, FI], F32, kind="ExternalInput").ap()
    wprT_h = nc.dram_tensor("wprT", [FI, D], F32, kind="ExternalInput").ap()
    identb_h = nc.dram_tensor("identb", [128, 128], BF16, kind="ExternalInput").ap()
    identf_h = nc.dram_tensor("identf", [128, 128], F32, kind="ExternalInput").ap()
    m64_h = nc.dram_tensor("m64", [64, 64], F32, kind="ExternalInput").ap()
    cmask_h = nc.dram_tensor("cmask", [128, T], BF16, kind="ExternalInput").ap()
    consts_h = nc.dram_tensor("consts", [128, 8], F32, kind="ExternalInput").ap()
    onesrow_h = nc.dram_tensor("onesrow", [1, 128], F32, kind="ExternalInput").ap()
    smask_h = nc.dram_tensor("smask", [128, 8], F32, kind="ExternalInput").ap()
    y_h = nc.dram_tensor("y", [T, D], F32, kind="ExternalOutput").ap()

    d_xn = nc.dram_tensor("d_xn", [16 * 128, T], BF16).ap()
    d_og = nc.dram_tensor("d_og", [H * 128, 16 * DV], BF16).ap()
    d_qk = nc.dram_tensor("d_qk", [H * 128, T], BF16).ap()
    d_h = nc.dram_tensor("d_h", [64 * 4 * 128, 512], BF16).ap()
    d_w8p = nc.dram_tensor("d_w8p", [4 * 128, 64 * 512], F8).ap()
    cc1_src = nc.dram_tensor("cc1_src", [1, 128], F32).ap()
    cc1_out = nc.dram_tensor("cc1_out", [1, 128], F32).ap()
    cc2_src = nc.dram_tensor("cc2_src", [128, 2056], F32).ap()
    cc2_out = nc.dram_tensor("cc2_out", [NCORE * 128, 2056], F32,
                             addr_space="Shared").ap()
    cch_src = nc.dram_tensor("cch_src", [1, 128], F32).ap()
    cch_out = nc.dram_tensor("cch_out", [1, 128], F32).ap()
    d_sp = nc.dram_tensor("d_sp", [128, H * DV], F32).ap()
    d_sbf = nc.dram_tensor("d_sbf", [128, 16 * T], BF16).ap()
    if debug:
        t_one1 = nc.dram_tensor("t_one1", [1, 16], F32, kind="ExternalOutput").ap()
        t_scal = nc.dram_tensor("t_scal", [128, 12], F32, kind="ExternalOutput").ap()
        t_xn = nc.dram_tensor("t_xn", [16 * 128, D], F32, kind="ExternalOutput").ap()
        t_og = nc.dram_tensor("t_og", [H * 128, 16 * DV], F32, kind="ExternalOutput").ap()
        t_qk = nc.dram_tensor("t_qk", [H * 128, T], F32, kind="ExternalOutput").ap()
        t_h = nc.dram_tensor("t_h", [2 * 128, 512], F32, kind="ExternalOutput").ap()
        t_cc2 = nc.dram_tensor("t_cc2", [NCORE * 128, 2056], F32, kind="ExternalOutput").ap()
        t_sbf = nc.dram_tensor("t_sbf", [128, 16 * T], F32, kind="ExternalOutput").ap()
        t_sprev = nc.dram_tensor("t_sprev", [128, H * DV], F32, kind="ExternalOutput").ap()
        t_sdump = nc.dram_tensor("t_sdump", [128, 64], F32, kind="ExternalOutput").ap()

    from contextlib import ExitStack
    es = ExitStack()

    def mk_sb(stack):
        return lambda n, s, d: stack.enter_context(nc.sbuf_tensor(n, s, d))
    sb = mk_sb(es)
    psa = lambda n, s, d: es.enter_context(nc.psum_tensor(n, s, d))

    with es:
        consts = sb("consts_s", [128, 8], F32)
        onesrow = sb("onesrow_s", [1, 128], F32)
        identb = sb("identb_s", [128, 128], BF16)
        identf = sb("identf_s", [128, 128], F32)
        m64 = sb("m64_s", [64, 64], F32)
        cmask = sb("cmask_s", [128, T], BF16)
        smask = sb("smask_s", [128, 8], F32)
        lnwb = sb("lnwb", [128, DV], F32)
        lnbb = sb("lnbb", [128, DV], F32)
        lnw_row = sb("lnw_row", [1, DV], F32)
        lnb_row = sb("lnb_row", [1, DV], F32)
        bgcol = sb("bgcol", [128, H], F32)
        scal = sb("scal", [128, 12], F32)
        SC_RQX, SC_RQFC, SC_RQPR, SC_SQF, SC_C2, SC_C1, SC_SH, SC_SPR = range(8)
        one1 = sb("one1", [1, 16], F32)
        (O_XM, O_GX, O_SX, O_RQX, O_WFC, O_SFC, O_RQFC, O_WPR, O_SPR, O_RQPR,
         O_C1, O_RM, O_H2L, O_T0, O_T1, O_T2) = range(16)
        cc1row = sb("cc1row", [1, 128], F32)
        xmax_c = sb("xmax_c", [128, 4], F32)
        t1 = sb("t1", [16, T], BF16)
        stat_a = sb("stat_a", [128, 16], F32)
        stat_b = sb("stat_b", [128, 16], F32)
        stat_c = sb("stat_c", [128, 16], F32)
        red_t = sb("red_t", [128, 2], F32)
        ones256 = sb("ones256", [128, DV], F32)
        zero8 = sb("zero8", [128, 8], F32)
        h2col = sb("h2col", [128, 8], F32)
        wacc = sb("wacc", [1, 4], F32)
        wred = sb("wred", [1, 4], F32)
        sdump = sb("sdump", [128, 64], F32) if debug else None

        PB = [psa(f"pb{i}", [128, 512], F32) for i in range(4)]
        PBb = [psa(f"pbb{i}", [128, 1024], BF16) for i in range(2)]
        PB2 = [psa(f"pb2_{i}", [128, 512], F32) for i in range(2)]

        sems = [Ctr(es.enter_context(nc.semaphore(n)), n)
                for n in ("qS", "qC", "qG", "PE", "V", "A", "G", "CC")]
        qS, qC, qG, sPE, sV, sA, sG, sCC = sems

        def mksem(n):
            c = Ctr(es.enter_context(nc.semaphore(n)), n)
            sems.append(c)
            return c
        qX = [mksem("qX0"), mksem("qX1")]
        qSL = [mksem("qSL0"), mksem("qSL1")]
        qWL = [mksem("qWL0"), mksem("qWL1")]
        qXN = [mksem(f"qXN{i}") for i in range(4)]
        qW8 = mksem("qW8")
        qDH = [mksem("qDH0"), mksem("qDH1")]
        qSPo, qSPq, qSPs = mksem("qSPo"), mksem("qSPq"), mksem("qSPs")
        qC1 = mksem("qC1")
        qHM = mksem("qHM")
        qRO, qRQ = mksem("qRO"), mksem("qRQ")
        qWO = mksem("qWO")
        qGK1 = mksem("qGK1")
        qH6 = [mksem(f"qH6_{i}") for i in range(6)]
        qXF = [mksem("qXF0"), mksem("qXF1")]
        qY = [mksem("qY0"), mksem("qY1")]
        qSP2 = mksem("qSP2")
        qSB = [mksem("qSB0"), mksem("qSB1")]
        qSBL = [mksem("qSBL0"), mksem("qSBL1")]
        qW2 = [mksem("qW2_0"), mksem("qW2_1")]
        P = Prog()
        bankw = {}

        def bw(key):
            v = bankw.get(key)
            return v if v is not None else (None, None)

        for dst, src in ((consts, consts_h), (onesrow, onesrow_h),
                         (identb, identb_h), (identf, identf_h), (m64, m64_h),
                         (cmask, cmask_h), (smask, smask_h), (lnw_row, lnw_h),
                         (lnb_row, lnb_h), (bgcol, bgk2_h)):
            P.op("sync", lambda e, dst=dst, src=src: e.dma_start(
                out=dst[:], in_=src[:]), inc=qS, inc_k=16)
        init_qs = qS.n

        P.op("tensor", lambda e: e.matmul(PB2[0][:, 0:DV], onesrow[0:1, :],
                                          lnw_row[0:1, :], start=True, stop=True),
             waits=[(qS, init_qs)])
        n0 = P.op("tensor", lambda e: e.matmul(PB2[1][:, 0:DV], onesrow[0:1, :],
                                               lnb_row[0:1, :], start=True, stop=True),
                  inc=sPE)
        P.op("vector", lambda e: e.tensor_copy(lnwb[:], PB2[0][:, 0:DV]),
             waits=[(sPE, n0)])
        P.op("vector", lambda e: e.tensor_copy(lnbb[:], PB2[1][:, 0:DV]))
        P.op("vector", lambda e: e.memset(xmax_c[:], 0.0))
        P.op("vector", lambda e: e.memset(ones256[:], 1.0))
        zr_v = P.op("vector", lambda e: e.memset(zero8[:], 0.0), inc=sV)
        bankw["P2_0"] = (sV, zr_v)
        bankw["P2_1"] = (sV, zr_v)

        esW = es.enter_context(ExitStack())
        sbW = mk_sb(esW)
        wsl = [sbW("wsl0", [128, 16 * 256], F32), sbW("wsl1", [128, 16 * 256], F32)]
        xsl = [sbW("xsl0", [128, 16 * 512], BF16), sbW("xsl1", [128, 16 * 512], BF16)]
        w8q = sbW("w8q", [128, 2 * 2048], F8)
        wgk1s = sbW("wgk1s", [128, 16 * 16], BF16)
        wqs = sbW("wqs", [128, 16 * 128], BF16)
        wks = sbW("wks", [128, 16 * 128], BF16)
        wvgs = sbW("wvgs", [128, 16 * 512], BF16)
        wgk2s = sbW("wgk2s", [16, 128], BF16)

        esA = esW.enter_context(ExitStack())
        sbA = mk_sb(esA)
        xt = [sbA("xt0", [128, D], F32), sbA("xt1", [128, D], F32)]
        lnsc = sbA("lnsc", [128, D], F32)
        xnb = sbA("xnb", [128, D], BF16)
        stage = sbA("stage", [128, 16 * 512], BF16)
        sqd = sbA("sqd", [128, D], F32)

        ln_done, st_done, pe_tr_last, p0_vrb = {}, {}, {}, {}
        for tt in range(16):
            buf = xt[tt % 2]
            qs_x = P.op("sync", lambda e, buf=buf, tt=tt: e.dma_start(
                out=buf[:], in_=x_h[tt * 128:(tt + 1) * 128, :]),
                waits=[(sV, ln_done.get(tt - 2)), (qS, init_qs if tt < 2 else 0)],
                inc=qX[tt % 2], inc_k=16)
            xw = (qX[tt % 2], qs_x)
            a_sq = P.op("scalar", lambda e, buf=buf, tt=tt: e.activation(
                sqd[:], buf[:], AF.Square),
                waits=[xw, (sV, p0_vrb.get(tt - 1))], inc=sA)
            v_r1 = P.op("vector", lambda e, buf=buf, tt=tt: e.tensor_reduce(
                stat_a[:, tt:tt + 1], buf[:], AXX, ALU.add),
                waits=[xw], inc=sV)
            v_mu = P.op("vector", lambda e, tt=tt: e.tensor_scalar_mul(
                stat_a[:, tt:tt + 1], stat_a[:, tt:tt + 1], 1.0 / D),
                waits=[(sV, v_r1)], inc=sV)
            v_rb = P.op("vector", lambda e, tt=tt: e.tensor_reduce(
                stat_b[:, tt:tt + 1], sqd[:], AXX, ALU.add),
                waits=[(sA, a_sq)], inc=sV)
            p0_vrb[tt] = v_rb
            v_ms = P.op("vector", lambda e, tt=tt: e.tensor_scalar_mul(
                stat_b[:, tt:tt + 1], stat_b[:, tt:tt + 1], 1.0 / D),
                waits=[(sV, v_rb)], inc=sV)
            v_m2 = P.op("vector", lambda e, tt=tt: e.tensor_scalar_mul(
                stat_c[:, tt:tt + 1], stat_a[:, tt:tt + 1],
                stat_a[:, tt:tt + 1]), waits=[(sV, v_mu)], inc=sV)
            v_var = P.op("vector", lambda e, tt=tt: e.tensor_sub(
                stat_c[:, tt:tt + 1], stat_b[:, tt:tt + 1], stat_c[:, tt:tt + 1]),
                waits=[(sV, v_ms), (sV, v_m2)], inc=sV)
            if debug:
                P.op("vector", lambda e, tt=tt: e.tensor_copy(
                    sdump[:, 4 * tt:4 * tt + 1], stat_a[:, tt:tt + 1]))
                P.op("vector", lambda e, tt=tt: e.tensor_copy(
                    sdump[:, 4 * tt + 1:4 * tt + 2], stat_b[:, tt:tt + 1]))
                P.op("vector", lambda e, tt=tt: e.tensor_copy(
                    sdump[:, 4 * tt + 2:4 * tt + 3], stat_c[:, tt:tt + 1]))
            P.op("scalar", lambda e, tt=tt: e.activation(
                stat_c[:, tt:tt + 1], stat_c[:, tt:tt + 1], AF.Ln,
                bias=consts[:, C_EPS:C_EPS + 1]), waits=[(sV, v_var)])
            a_rs = P.op("scalar", lambda e, tt=tt: e.activation(
                stat_c[:, tt:tt + 1], stat_c[:, tt:tt + 1], AF.Exp,
                scale=consts[:, C_M05:C_M05 + 1]), inc=sA)
            if debug:
                P.op("vector", lambda e, tt=tt: e.tensor_copy(
                    sdump[:, 4 * tt + 3:4 * tt + 4], stat_c[:, tt:tt + 1]),
                    waits=[(sA, a_rs)])
            v_ln = P.op("vector", lambda e, buf=buf, tt=tt: e.tensor_scalar(
                lnsc[:], buf[:], stat_a[:, tt:tt + 1], stat_c[:, tt:tt + 1],
                ALU.subtract, ALU.mult), waits=[(sA, a_rs)], inc=sV)
            v_r2 = P.op("vector", lambda e, tt=tt: e.tensor_reduce(
                stat_b[:, tt:tt + 1], lnsc[:], AXX, ALU.max,
                apply_absolute_value=True), waits=[(sV, v_ln)], inc=sV)
            P.op("vector", lambda e, tt=tt: e.tensor_max(
                xmax_c[:, 0:1], xmax_c[:, 0:1], stat_b[:, tt:tt + 1]),
                waits=[(sV, v_r2)])
            v_xnb = P.op("vector", lambda e: e.tensor_copy(xnb[:], lnsc[:]),
                         waits=[(sPE, pe_tr_last.get(tt - 1)), (sV, v_ln)],
                         inc=sV)
            st = tt % 4
            for kb in range(16):
                key = f"Pb_{kb % 2}"
                pe_tr = P.op("tensor", lambda e, kb=kb: e.transpose(
                    PBb[kb % 2][:, 0:128], xnb[:, kb * 128:(kb + 1) * 128],
                    identb[:]), waits=[(sV, v_xnb), bw(key)], inc=sPE)
                pe_tr_last[tt] = pe_tr
                bankw[key] = (sV, P.op(
                    "vector", lambda e, kb=kb, st=st: e.tensor_copy(
                        stage[:, kb * 512 + st * 128:kb * 512 + (st + 1) * 128],
                        PBb[kb % 2][:, 0:128]),
                    waits=[(sPE, pe_tr)] +
                    ([(qXN[tt // 4 - 1], st_done.get(tt - 1))]
                     if (kb == 0 and st == 0 and tt >= 4)
                     else []), inc=sV))
            ln_done[tt] = sV.n
            if st == 3:
                g = tt // 4
                st_done[g * 4 + 3] = P.op("sync", lambda e, g=g: e.dma_start(
                    out=d_xn.rearrange("(kb p) t -> p kb t", p=128)
                    [:, :, g * 512:(g + 1) * 512],
                    in_=stage[:].rearrange("p (kb q) -> p kb q", kb=16)),
                    waits=[(sV, ln_done[tt])], inc=qXN[g], inc_k=16)
        p0_qs = qS.n
        p0_v = sV.n

        xch = {}

        def emit_xmax_launch():
            pe_xm = P.op("tensor", lambda e: e.transpose(
                PB2[0][0:1, 0:128], xmax_c[:, 0:1], identf[:]),
                waits=[(sV, p0_v), bw("P2_0")], inc=sPE)
            v_xm = P.op("vector", lambda e: e.tensor_reduce(
                one1[0:1, O_XM:O_XM + 1], PB2[0][0:1, 0:128], AXX, ALU.max),
                waits=[(sPE, pe_xm)], inc=sV)
            bankw["P2_0"] = (sV, v_xm)
            v_cc1 = P.op("vector", lambda e: e.tensor_scalar_mul(
                cc1row[0:1, :], onesrow[0:1, :], one1[0:1, O_XM:O_XM + 1]),
                waits=[(sV, v_xm)], inc=sV)
            qg_cc1 = P.op("gpsimd", lambda e: e.dma_start(
                out=cc1_src[:], in_=cc1row[:]),
                waits=[(sV, v_cc1)], inc=qC1, inc_k=16)
            xch["cc1_n"] = P.op("gpsimd", lambda e: e.collective_compute(
                "AllReduce", ALU.max, replica_groups=[list(range(NCORE))],
                ins=[cc1_src.opt()], outs=[cc1_out.opt()]),
                waits=[(qC1, qg_cc1)], inc=sCC)

        def emit_xmax_tail():
            qg_gx = P.op("gpsimd", lambda e: e.dma_start(
                out=one1[0:1, O_GX:O_GX + 1], in_=cc1_out[0:1, 0:1]),
                waits=[(sCC, xch["cc1_n"])], inc=qC1, inc_k=16)
            v_sx = P.op("vector", lambda e: e.tensor_scalar(
                one1[0:1, O_SX:O_SX + 1], one1[0:1, O_GX:O_GX + 1],
                1.0 / 448.0, 1e-6, ALU.mult, ALU.max),
                waits=[(qC1, qg_gx)], inc=sV)
            v_t0 = P.op("vector", lambda e: e.tensor_scalar_mul(
                one1[0:1, O_T0:O_T0 + 1], one1[0:1, O_SX:O_SX + 1], 2.0),
                waits=[(sV, v_sx)], inc=sV)
            v_rq0 = P.op("vector", lambda e: e.reciprocal(
                one1[0:1, O_RQX:O_RQX + 1], one1[0:1, O_T0:O_T0 + 1]),
                waits=[(sV, v_t0)], inc=sV)
            xch["v_sx"] = v_sx
            xch["v_rqx"] = bcast_col(O_RQX, SC_RQX, v_rq0)

        def bcast_col(src_col, dst_col, wait_v):
            pe = P.op("tensor", lambda e, s=src_col: e.matmul(
                PB2[1][:, 0:1], onesrow[0:1, :], one1[0:1, s:s + 1],
                start=True, stop=True), waits=[(sV, wait_v), bw("P2_1")], inc=sPE)
            v = P.op("vector", lambda e, d=dst_col: e.tensor_copy(
                scal[:, d:d + 1], PB2[1][:, 0:1]), waits=[(sPE, pe)], inc=sV)
            bankw["P2_1"] = (sV, v)
            return v

        esA.close()

        wfcT_r = wfcT_h.rearrange("(a p) j -> p a j", p=128)
        wprT_r = wprT_h.rearrange("(a p) c -> p a c", p=128)
        d_w8p_st = d_w8p.rearrange("(c p) (mt q) -> p c mt q", p=128, q=512)
        wm_state = {"seq": 0, "cons": {}, "wpr_g": None, "wfc_g": None,
                    "w8_qc": {}, "p55_v": None, "vals": {}}

        def emit_wmax_slab(i):
            seq = wm_state["seq"]; wm_state["seq"] += 1
            buf = wsl[seq % 2]

            def ld(e, i=i, buf=buf):
                if i < 32:
                    return e.dma_start(
                        out=buf[:].rearrange("p (a j) -> p a j", a=16),
                        in_=wfcT_r[:, :, i * 256:(i + 1) * 256])
                return e.dma_start(
                    out=buf[:].rearrange("p (a j) -> p a j", a=2),
                    in_=wprT_r[:, (i - 32) * 2:(i - 31) * 2, :])
            qc_w = P.op("scalar", ld, waits=[(sV, wm_state["cons"].get(seq - 2))],
                        inc=qWL[seq % 2], inc_k=16)
            v_r3 = P.op("vector", lambda e, buf=buf: e.tensor_reduce(
                red_t[:, 0:1], buf[:], AXX, ALU.max, apply_absolute_value=True),
                waits=[(qWL[seq % 2], qc_w), (sV, zr_v)], inc=sV)
            col = 1 if i < 32 else 2
            g = P.op("vector", lambda e, col=col: e.tensor_max(
                xmax_c[:, col:col + 1], xmax_c[:, col:col + 1], red_t[:, 0:1]),
                waits=[(sV, v_r3)], inc=sV)
            wm_state["cons"][seq] = g
            if i == 31:
                wm_state["wfc_g"] = g
            if i == 63:
                wm_state["wpr_g"] = g

        def emit_wscale_tail(which):
            if which == "wpr":
                colx, omax, osc, orq, dcol, gv = 2, O_WPR, O_SPR, O_RQPR, SC_RQPR, wm_state["wpr_g"]
            else:
                colx, omax, osc, orq, dcol, gv = 1, O_WFC, O_SFC, O_RQFC, SC_RQFC, wm_state["wfc_g"]
            pe_w = P.op("tensor", lambda e, colx=colx: e.transpose(
                PB2[0][0:1, 0:128], xmax_c[:, colx:colx + 1], identf[:]),
                waits=[(sV, gv), bw("P2_0")], inc=sPE)
            v_w = P.op("vector", lambda e, omax=omax: e.tensor_reduce(
                one1[0:1, omax:omax + 1], PB2[0][0:1, 0:128], AXX, ALU.max),
                waits=[(sPE, pe_w)], inc=sV)
            bankw["P2_0"] = (sV, v_w)
            v_sc = P.op("vector", lambda e, omax=omax, osc=osc: e.tensor_scalar(
                one1[0:1, osc:osc + 1], one1[0:1, omax:omax + 1],
                1.0 / 448.0, 1e-6, ALU.mult, ALU.max), waits=[(sV, v_w)], inc=sV)
            v_t1 = P.op("vector", lambda e, osc=osc: e.tensor_scalar_mul(
                one1[0:1, O_T1:O_T1 + 1], one1[0:1, osc:osc + 1], 2.0),
                waits=[(sV, v_sc)], inc=sV)
            v_r = P.op("vector", lambda e, orq=orq: e.reciprocal(
                one1[0:1, orq:orq + 1], one1[0:1, O_T1:O_T1 + 1]),
                waits=[(sV, v_t1)], inc=sV)
            wm_state["vals"][dcol] = bcast_col(orq, dcol, v_r)
            if which == "wpr":
                wm_state["vals"][SC_SPR] = bcast_col(O_SPR, SC_SPR, v_r)
            else:
                v_c1a = P.op("vector", lambda e: e.tensor_scalar_mul(
                    one1[0:1, O_C1:O_C1 + 1], one1[0:1, O_SX:O_SX + 1], 4.0),
                    waits=[(sV, xch["v_rqx"]), (sV, xch["v_sx"])], inc=sV)
                v_c1 = P.op("vector", lambda e: e.tensor_mul(
                    one1[0:1, O_C1:O_C1 + 1], one1[0:1, O_C1:O_C1 + 1],
                    one1[0:1, O_SFC:O_SFC + 1]),
                    waits=[(sV, v_c1a), (sV, v_sc)], inc=sV)
                wm_state["vals"]["v_c1"] = v_c1
                wm_state["vals"][SC_C1] = bcast_col(O_C1, SC_C1, v_c1)

        def emit_p55_slab(msl):
            seq = wm_state["seq"]; wm_state["seq"] += 1
            buf = wsl[seq % 2]
            qc_w = P.op("scalar", lambda e, msl=msl, buf=buf: e.dma_start(
                out=buf[:].rearrange("p (a j) -> p a j", a=2),
                in_=wprT_r[:, msl * 2:(msl + 1) * 2, :]),
                waits=[(sV, wm_state["cons"].get(seq - 2))],
                inc=qWL[seq % 2], inc_k=16)
            v_q = P.op("vector", lambda e, buf=buf: e.tensor_scalar_mul(
                w8q[:].rearrange("p (c a q) -> p a c q", c=4, a=2),
                buf[:].rearrange("p (a c q) -> p a c q", a=2, c=4),
                scal[:, SC_RQPR:SC_RQPR + 1]),
                waits=[(qWL[seq % 2], qc_w), (sV, wm_state["vals"][SC_RQPR]),
                       (qW8, wm_state["w8_qc"].get(msl - 1))], inc=sV)
            wm_state["cons"][seq] = v_q
            wm_state["p55_v"] = v_q
            wm_state["w8_qc"][msl] = P.op("scalar", lambda e, msl=msl: e.dma_start(
                out=d_w8p_st[:, :, msl * 2:(msl + 1) * 2, :],
                in_=w8q[:].rearrange("p (c a q) -> p c a q", c=4, a=2)),
                waits=[(sV, v_q)], inc=qW8, inc_k=16)

        qg_wgk1 = P.op("gpsimd", lambda e: e.dma_start(
            out=wgk1s[:].rearrange("p (a c) -> p a c", a=16),
            in_=wgk1_h.rearrange("(a p) c -> p a c", p=128)), inc=qGK1, inc_k=16)

        d_xn_ld = d_xn.rearrange("(kb p) t -> p kb t", p=128)
        xsl_seq = [0]

        def load_xsl(ts4):
            seq = xsl_seq[0]
            buf = xsl[seq % 2]
            g = bw(("xsl", seq % 2))
            qv = P.op("sync", lambda e, ts4=ts4, buf=buf: e.dma_start(
                out=buf[:].rearrange("p (kb q) -> p kb q", kb=16),
                in_=d_xn_ld[:, :, ts4 * 512:(ts4 + 1) * 512]),
                waits=[(qXN[ts4], st_done[ts4 * 4 + 3]), g],
                inc=qSL[seq % 2], inc_k=16)
            xsl_seq[0] += 1
            return seq, buf, (qSL[seq % 2], qv)

        def mark_xsl(seq, ctr, val):
            bankw[("xsl", seq % 2)] = (ctr, val)

        t1_v = None
        for ts4 in range(4):
            seq, buf, qs_x = load_xsl(ts4)

            def mmt1(e, buf=buf, ts4=ts4):
                for kb in range(16):
                    last = e.matmul(PB[ts4 % 2][0:16, 0:512],
                                    wgk1s[:, kb * 16:(kb + 1) * 16],
                                    buf[:, kb * 512:(kb + 1) * 512],
                                    start=(kb == 0), stop=(kb == 15))
                return last
            pe_t1 = P.op("tensor", mmt1,
                         waits=[qs_x, (qGK1, qg_wgk1), bw(f"P{ts4 % 2}")],
                         inc=sPE)
            mark_xsl(seq, sPE, pe_t1)
            t1_v = P.op("vector", lambda e, ts4=ts4: e.tensor_copy(
                t1[:, ts4 * 512:(ts4 + 1) * 512], PB[ts4 % 2][0:16, 0:512]),
                waits=[(sPE, pe_t1)], inc=sV)
            bankw[f"P{ts4 % 2}"] = (sV, t1_v)

        esB = esW.enter_context(ExitStack())
        sbB = mk_sb(esB)
        qT = sbB("qT", [128, T], BF16)
        kT = sbB("kT", [128, T], BF16)
        vg = sbB("vg", [128, 16 * 512], BF16)
        scrA = sbB("scrA", [128, T], F32)
        scrB = sbB("scrB", [128, T], F32)
        gkT = sbB("gkT", [128, T], BF16)
        e2b = sbB("e2b", [128, T], BF16)
        sqt = sbB("sqt", [128, T], BF16)
        skt = sbB("skt", [128, T], BF16)
        qtp = sbB("qtp", [128, 2 * CH], BF16)
        qkl = sbB("qkl", [128, T], BF16)
        k2a = sbB("k2a", [128, 16 * 128], BF16)
        dall = sbB("dall", [128, 128 * NCH], BF16)
        sall = sbB("sall", [128, 128 * NCH], BF16)
        ebz = sbB("ebz", [128, 128 * NCH], BF16)
        scp = sbB("scp", [128, 2 * DV], BF16)
        pse = sbB("pse", [64, 2 * 64], BF16)
        cscol = sbB("cscol", [128, 16], F32)
        ogasm = sbB("ogasm", [128, 16 * DV], BF16)
        eball = sbB("eball", [128, NCH], F32)
        cpall = sbB("cpall", [128, NCH], F32)
        kapall = sbB("kapall", [128, NCH], F32)
        onesc = sbB("onesc", [128, NCH], F32)
        sfin = sbB("sfin", [128, DV], F32)

        dall3 = dall[:].rearrange("p (e i) -> p e i", i=NCH)
        sall3 = sall[:].rearrange("p (e i) -> p e i", i=NCH)
        ebz3 = ebz[:].rearrange("p (e i) -> p e i", i=NCH)
        Bc3 = scrB[:].rearrange("p (i c) -> p i c", c=CH)
        d_og_r = d_og.rearrange("(h p) f -> h p f", p=128)
        d_qk_r = d_qk.rearrange("(h p) f -> h p f", p=128)

        v_ones = P.op("vector", lambda e: e.memset(onesc[:], 1.0), inc=sV)

        head_pe, a_abs = {}, {}
        spill_og, spill_qk, spill_sf = {}, {}, {}
        head_gate = t1_v
        v_scan_prev = None
        for h in range(H):
            slab_jobs = [
                (lambda k=k: emit_wmax_slab(
                    (32 + 8 * h + k) if h < 4 else (8 * (h - 4) + k)))
                for k in range(8)]
            if 4 <= h:
                slab_jobs += [(lambda k=k: emit_p55_slab(8 * (h - 4) + k))
                              for k in range(8)]

            def do_slabs(n):
                for _ in range(n):
                    if slab_jobs:
                        slab_jobs.pop(0)()
            for (dst, src) in ((wqs, wq_h[h * 128:(h + 1) * 128, :]),
                               (wks, wk_h[h * 128:(h + 1) * 128, :]),
                               (wvgs, wvg_h[h * 128:(h + 1) * 128, :])):
                P.op("gpsimd", lambda e, dst=dst, src=src: e.dma_start(
                    out=dst[:], in_=src),
                    waits=[(sPE, head_pe.get(h - 1))],
                    inc=qG, inc_k=16)
            qg_w = P.op("gpsimd", lambda e, h=h: e.dma_start(
                out=wgk2s[:], in_=wgk2_h[:, h * 128:(h + 1) * 128]),
                inc=qG, inc_k=16)
            if h == 0:
                emit_xmax_launch()

            for ts4 in range(4):
                seq, buf, qs_x = load_xsl(ts4)

                def mmq(e, buf=buf):
                    for kb in range(16):
                        last = e.matmul(PB[0][:, 0:512],
                                        wqs[:, kb * 128:(kb + 1) * 128],
                                        buf[:, kb * 512:(kb + 1) * 512],
                                        start=(kb == 0), stop=(kb == 15))
                    return last
                pe_q = P.op("tensor", mmq,
                            waits=[qs_x, (qG, qg_w), bw("P0"),
                                   (sV, head_gate)], inc=sPE)
                ev_q = P.op("vector", lambda e, ts4=ts4: e.tensor_copy(
                    qT[:, ts4 * 512:(ts4 + 1) * 512], PB[0][:, 0:512]),
                    waits=[(sPE, pe_q)], inc=sV)
                bankw["P0"] = (sV, ev_q)

                def mmk(e, buf=buf):
                    for kb in range(16):
                        last = e.matmul(PB[1][:, 0:512],
                                        wks[:, kb * 128:(kb + 1) * 128],
                                        buf[:, kb * 512:(kb + 1) * 512],
                                        start=(kb == 0), stop=(kb == 15))
                    return last
                pe_k = P.op("tensor", mmk, waits=[bw("P1")], inc=sPE)
                ev_k = P.op("vector", lambda e, ts4=ts4: e.tensor_copy(
                    kT[:, ts4 * 512:(ts4 + 1) * 512], PB[1][:, 0:512]),
                    waits=[(sPE, pe_k)], inc=sV)
                bankw["P1"] = (sV, ev_k)

                for st in range(4):
                    pbk = 2 + (st % 2)

                    def mmvg(e, buf=buf, st=st, pbk=pbk):
                        for kb in range(16):
                            last = e.matmul(
                                PB[pbk][:, 0:512],
                                buf[:, kb * 512 + st * 128: kb * 512 + (st + 1) * 128],
                                wvgs[:, kb * 512:(kb + 1) * 512],
                                start=(kb == 0), stop=(kb == 15))
                        return last
                    pe_vg = P.op("tensor", mmvg, waits=[bw(f"P{pbk}")], inc=sPE)
                    ev_vg = P.op("vector", lambda e, ts4=ts4, st=st, pbk=pbk:
                                 e.tensor_copy(
                        vg[:, (ts4 * 4 + st) * 512:(ts4 * 4 + st + 1) * 512],
                        PB[pbk][:, 0:512]), waits=[(sPE, pe_vg)], inc=sV)
                    bankw[f"P{pbk}"] = (sV, ev_vg)

                pe_g = P.op("tensor", lambda e, ts4=ts4: e.matmul(
                    PB[0][:, 0:512], wgk2s[:, :], t1[:, ts4 * 512:(ts4 + 1) * 512],
                    start=True, stop=True), waits=[bw("P0")], inc=sPE)
                mark_xsl(seq, sPE, pe_g)
                ev_g = P.op("vector", lambda e, ts4=ts4, h=h: e.tensor_scalar_add(
                    scrA[:, ts4 * 512:(ts4 + 1) * 512], PB[0][:, 0:512],
                    bgcol[:, h:h + 1]),
                    waits=[(sPE, pe_g), (sA, a_abs.get(h - 1))], inc=sV)
                bankw["P0"] = (sV, ev_g)
                do_slabs(2 if h < 4 else 3)
            v_proj = sV.n

            a_abs[h] = P.op("scalar", lambda e: e.activation(
                scrB[:], scrA[:], AF.Abs), waits=[(sV, v_proj)], inc=sA)
            P.op("scalar", lambda e: e.activation(
                scrB[:], scrB[:], AF.Exp, scale=consts[:, C_M1:C_M1 + 1]))
            a_ls = P.op("scalar", lambda e: e.activation(
                scrB[:], scrB[:], AF.Ln, bias=1.0), inc=sA)
            P.op("vector", lambda e: e.tensor_scalar(
                scrA[:], scrA[:], 0.0, 1.0 / 16.0, ALU.min, ALU.mult),
                waits=[(sA, a_ls)])
            v_gk = P.op("vector", lambda e: e.scalar_tensor_tensor(
                gkT[:], scrB[:], -1.0 / 16.0, scrA[:], ALU.mult, ALU.add), inc=sV)
            v_bc = P.op("vector", lambda e: e.tensor_tensor_scan(
                scrB[:], cmask[:], gkT[:], 0.0, ALU.mult, ALU.add),
                waits=[(sV, v_gk)], inc=sV)

            a_e2 = P.op("scalar", lambda e: e.activation(
                e2b[:], scrB[:], AF.Exp), waits=[(sV, v_bc)], inc=sA)
            a_eb = P.op("scalar", lambda e: e.activation(
                eball[:], Bc3[:, :, CH - 1], AF.Exp), inc=sA)

            v_cp = P.op("vector", lambda e: e.tensor_tensor_scan(
                cpall[:], eball[:], onesc[:], 1.0, ALU.mult, ALU.mult),
                waits=[(sA, a_eb), (sV, v_ones)], inc=sV)
            v_kap = P.op("vector", lambda e: e.tensor_mul(
                kapall[:], eball[:], cpall[:]), waits=[(sV, v_cp)], inc=sV)

            v_sq = P.op("vector", lambda e: e.tensor_mul(
                sqt[:], qT[:], e2b[:]), waits=[(sA, a_e2)], inc=sV)
            a_e3 = P.op("scalar", lambda e: e.activation(
                e2b[:], scrB[:], AF.Exp,
                scale=consts[:, C_M1:C_M1 + 1]), waits=[(sV, v_sq)], inc=sA)
            v_sk = P.op("vector", lambda e: e.tensor_mul(
                skt[:], kT[:], e2b[:]), waits=[(sA, a_e3)], inc=sV)

            ilast = None
            for i in range(NCH):
                ilast = P.op("vector", lambda e, i=i: e.tensor_scalar_mul(
                    qkl[:, i * CH:(i + 1) * CH], qT[:, i * CH:(i + 1) * CH],
                    kapall[:, i:i + 1]),
                    waits=[(sV, v_kap), (qSPq, spill_qk.get(h - 1))] if i == 0 else [],
                    inc=sV if i == NCH - 1 else None)
            v_qkl = ilast

            for i in range(NCH):
                key = f"P{i % 2}"
                pe_p = P.op("tensor", lambda e, i=i: e.matmul(
                    PB[i % 2][0:64, 0:64], skt[:, i * CH:(i + 1) * CH],
                    sqt[:, i * CH:(i + 1) * CH], start=True, stop=True),
                    waits=[(sV, v_sq if i == 0 else None),
                           (sV, v_sk if i == 0 else None), bw(key)], inc=sPE)
                v_pe = P.op("vector", lambda e, i=i: e.tensor_mul(
                    pse[0:64, (i % 2) * 64:(i % 2 + 1) * 64],
                    PB[i % 2][0:64, 0:64], m64[:]), waits=[(sPE, pe_p)], inc=sV)
                bankw[key] = (sV, v_pe)
                P.op("vector", lambda e, i=i: e.tensor_reduce(
                    cscol[64 * (i % 2):64 * (i % 2) + 64, i // 2:i // 2 + 1],
                    pse[0:64, (i % 2) * 64:(i % 2 + 1) * 64], AXX, ALU.add),
                    waits=[(sV, v_pe)], inc=sV if i == NCH - 1 else None)
            v_cs = sV.n
            do_slabs(2)

            for tt in range(16):
                ilast = P.op("vector", lambda e, tt=tt: e.scalar_tensor_tensor(
                    ogasm[:, tt * DV:(tt + 1) * DV],
                    vg[:, tt * 512:tt * 512 + DV], cscol[:, tt:tt + 1],
                    vg[:, tt * 512 + DV:(tt + 1) * 512], ALU.mult, ALU.add),
                    waits=([(sV, v_cs), (qSPo, spill_og.get(h - 1))]
                           if tt == 0 else []),
                    inc=sV if tt == 15 else None)
            v_og0 = ilast

            for i in range(NCH):
                key = f"Pb_{i % 2}"
                pe_t = P.op("tensor", lambda e, i=i: e.transpose(
                    PBb[i % 2][0:64, 0:128], skt[:, i * CH:(i + 1) * CH],
                    identb[:]),
                    waits=[(sV, v_sk if i == 0 else None), bw(key)], inc=sPE)
                bankw[key] = (sV, P.op("vector", lambda e, i=i: e.tensor_copy(
                    k2a[64 * (i % 2):64 * (i % 2) + 64,
                        (i // 2) * 128:(i // 2 + 1) * 128],
                    PBb[i % 2][0:64, 0:128]), waits=[(sPE, pe_t)], inc=sV))
            v_k2 = sV.n
            do_slabs(len(slab_jobs))

            v_sfh = None
            for eh in range(2):
                P.op("vector", lambda e: e.memset(ebz3[:, :, 0], 0.0),
                     waits=[(sV, v_scan_prev)])
                ilast = None
                for i in range(1, NCH):
                    ilast = P.op("vector", lambda e, i=i: e.tensor_scalar_mul(
                        ebz3[:, :, i], ones256[:, 0:128], eball[:, i:i + 1]),
                        waits=[(sA, a_eb)] if i == 1 else [],
                        inc=sV if i == NCH - 1 else None)
                v_ebz = ilast

                for i in range(NCH):
                    key = f"P{2 + i % 2}"
                    pe_d = P.op("tensor", lambda e, i=i: e.matmul(
                        PB[2 + i % 2][:, 0:DV],
                        k2a[64 * (i % 2):64 * (i % 2) + 64,
                            (i // 2) * 128:(i // 2 + 1) * 128],
                        vg[64 * (i % 2):64 * (i % 2) + 64,
                           (i // 2) * 512:(i // 2) * 512 + DV],
                        start=True, stop=True),
                        waits=[(sV, v_k2 if (i == 0 and eh == 0) else None),
                               bw(key)], inc=sPE)
                    bankw[key] = (sV, P.op(
                        "vector", lambda e, i=i, eh=eh: e.tensor_scalar_mul(
                            dall3[:, :, i],
                            PB[2 + i % 2][:, eh * 128:(eh + 1) * 128],
                            eball[:, i:i + 1]),
                        waits=[(sPE, pe_d)], inc=sV))
                v_dall = sV.n

                v_scan = P.op("vector", lambda e: e.tensor_tensor_scan(
                    sall[:], ebz[:], dall[:], 0.0, ALU.mult, ALU.add),
                    waits=[(sV, v_ebz), (sV, v_dall)], inc=sV)
                v_scan_prev = v_scan

                o_pe, sc_v = {}, {}
                for i in range(NCH):
                    sc_v[i] = P.op("vector", lambda e, i=i, eh=eh: e.tensor_copy(
                        scp[:, (i % 2) * DV + eh * 128:
                            (i % 2) * DV + (eh + 1) * 128], sall3[:, :, i]),
                        waits=([(sV, v_scan)] if i == 0 else []) +
                        [(sPE, o_pe.get(i - 2))], inc=sV)
                    v_qt = P.op("vector", lambda e, i=i: e.tensor_scalar_mul(
                        qtp[:, (i % 2) * CH:(i % 2 + 1) * CH],
                        qT[:, i * CH:(i + 1) * CH], eball[:, i:i + 1]),
                        waits=[(sPE, o_pe.get(i - 2))], inc=sV)
                    key = f"P{i % 2}"
                    o_pe[i] = P.op("tensor", lambda e, i=i, eh=eh: e.matmul(
                        PB[i % 2][0:64, eh * 128:(eh + 1) * 128],
                        qtp[:, (i % 2) * CH:(i % 2 + 1) * CH],
                        scp[:, (i % 2) * DV + eh * 128:
                            (i % 2) * DV + (eh + 1) * 128],
                        start=True, stop=True),
                        waits=[(sV, sc_v[i]), (sV, v_qt), bw(key)], inc=sPE)
                    bankw[key] = (sV, P.op(
                        "vector", lambda e, i=i, eh=eh: e.tensor_add(
                            ogasm[64 * (i % 2):64 * (i % 2) + 64,
                                  (i // 2) * DV + eh * 128:
                                  (i // 2) * DV + (eh + 1) * 128],
                            PB[i % 2][0:64, eh * 128:(eh + 1) * 128],
                            ogasm[64 * (i % 2):64 * (i % 2) + 64,
                                  (i // 2) * DV + eh * 128:
                                  (i // 2) * DV + (eh + 1) * 128]),
                        waits=[(sPE, o_pe[i]),
                               (sV, v_og0 if (i == 0 and eh == 0) else None)],
                        inc=sV))
                v_sfh = P.op("vector", lambda e, eh=eh: e.tensor_copy(
                    sfin[:, eh * 128:(eh + 1) * 128],
                    scp[:, ((NCH - 1) % 2) * DV + eh * 128:
                        ((NCH - 1) % 2) * DV + (eh + 1) * 128]),
                    waits=[(sV, sc_v[NCH - 1]),
                           (qSPs, spill_sf.get(h - 1) if eh == 0 else None)],
                    inc=sV)
            v_ogdone = sV.n
            v_sf = v_sfh

            spill_og[h] = P.op("scalar", lambda e, h=h: e.dma_start(
                out=d_og_r[h], in_=ogasm[:]), waits=[(sV, v_ogdone)],
                inc=qSPo, inc_k=16)
            spill_qk[h] = P.op("scalar", lambda e, h=h: e.dma_start(
                out=d_qk_r[h], in_=qkl[:]), waits=[(sV, v_qkl)],
                inc=qSPq, inc_k=16)
            spill_sf[h] = P.op("scalar", lambda e, h=h: e.dma_start(
                out=cc2_src[:, h * DV:(h + 1) * DV], in_=sfin[:]),
                waits=[(sV, v_sf)], inc=qSPs, inc_k=16)

            head_pe[h] = sPE.n
            head_gate = sV.n
            if h == 0:
                emit_xmax_tail()
            if h == 3:
                emit_wscale_tail("wpr")
            if h == 7:
                emit_wscale_tail("wfc")
        v_rqx = xch["v_rqx"]
        v_sprb = wm_state["vals"][SC_SPR]
        v_c1 = wm_state["vals"]["v_c1"]
        v_c1b = wm_state["vals"][SC_C1]
        p15_v = wm_state["vals"][SC_RQFC]
        p55_v = wm_state["p55_v"]
        w8p_qc = qW8.n
        p3_qc = qC.n
        p3_pe = sPE.n
        p3_v = sV.n

        esB.close()

        cc2_n = P.op("gpsimd", lambda e: e.collective_compute(
            "AllGather", ALU.bypass, replica_groups=[list(range(NCORE))],
            ins=[cc2_src.opt()], outs=[cc2_out.opt()]),
            waits=[(qSPs, qSPs.n)], inc=sCC)
        cc2_blk = cc2_out.rearrange("(r p) c -> r p c", p=128)

        esP7 = esW.enter_context(ExitStack())
        sbP7 = mk_sb(esP7)
        spf32 = sbP7("spf32", [128, H * DV], F32)
        g_r = sbP7("g_r", [128, H * DV], F32)
        p7_state = {"v_sp": None, "v_spb": None}

        def emit_p7_step(r):
            v_sp = p7_state["v_sp"]
            qg_g = P.op("gpsimd", lambda e, r=r: e.dma_start(
                out=g_r[:], in_=cc2_blk[r, :, 0:2048]),
                waits=[(sCC, cc2_n), (sV, v_sp)], inc=qG, inc_k=16)
            if r == 0:
                v_sp = P.op("vector", lambda e: e.tensor_scalar_mul(
                    spf32[:], g_r[:], smask[:, 0:1]),
                    waits=[(qG, qg_g)], inc=sV)
            else:
                v_sp = P.op("vector", lambda e, r=r: e.scalar_tensor_tensor(
                    spf32[:], g_r[:], smask[:, r:r + 1], spf32[:],
                    ALU.mult, ALU.add), waits=[(qG, qg_g), (sV, v_sp)], inc=sV)
            p7_state["v_sp"] = v_sp

        esC = esW.enter_context(ExitStack())
        sbC = mk_sb(esC)
        x8 = sbC("x8", [128, 16 * T], F8)
        w8f = [sbC("w8f0", [128, 16 * 256], F8), sbC("w8f1", [128, 16 * 256], F8)]
        rbuf = sbC("rbuf", [128, 2 * 512], BF16)

        x83 = x8[:].rearrange("p (kb t) -> p kb t", kb=16)
        v_x8 = None
        for ts4 in range(4):
            seq, buf, qs_x = load_xsl(ts4)
            v_x8 = P.op("vector", lambda e, ts4=ts4, buf=buf: e.tensor_scalar_mul(
                x83[:, :, ts4 * 512:(ts4 + 1) * 512],
                buf[:].rearrange("p (kb st q) -> p kb (st q)", kb=16, st=4),
                scal[:, SC_RQX:SC_RQX + 1]),
                waits=[qs_x, (sV, v_rqx),
                       (sV, p3_v if ts4 == 0 else None)],
                inc=sV)
            mark_xsl(seq, sV, v_x8)
        v_x8f = v_x8

        d_h_bl = d_h.rearrange("(jt tt p) q -> jt tt p q", tt=4, p=128)
        fc1_vq, fc1_pe = {}, {}
        rb_guard = {}
        for jsl in range(32):
            if 8 <= jsl < 16:
                emit_p7_step(jsl - 8)
            elif jsl == 16:
                p7_state["qg_sp"] = P.op(
                    "gpsimd", lambda e: e.dma_start(out=d_sp[:], in_=spf32[:]),
                    waits=[(sV, p7_state["v_sp"])], inc=qSP2, inc_k=16)
            buf = wsl[jsl % 2]
            qs_w = P.op("sync", lambda e, jsl=jsl, buf=buf: e.dma_start(
                out=buf[:].rearrange("p (a j) -> p a j", a=16),
                in_=wfcT_r[:, :, jsl * 256:(jsl + 1) * 256]),
                waits=[(sV, p55_v if jsl < 2 else fc1_vq.get(jsl - 2))],
                inc=qWL[jsl % 2], inc_k=16)
            w8fb = w8f[jsl % 2]
            fc1_vq[jsl] = P.op("vector", lambda e, buf=buf, w8fb=w8fb:
                               e.tensor_scalar_mul(
                w8fb[:], buf[:], scal[:, SC_RQFC:SC_RQFC + 1]),
                waits=[(qWL[jsl % 2], qs_w), (sV, p15_v), (sPE, fc1_pe.get(jsl - 2))],
                inc=sV)
            w8fb3 = w8fb[:].rearrange("p (a j) -> p a j", a=16)
            for jj in range(2):
                def mmfc(e, jj=jj, w8fb3=w8fb3):
                    for kt2 in range(8):
                        for tt in range(4):
                            last = e.matmul(
                                PB[tt][:, 0:512],
                                w8fb3[:, 2 * kt2:2 * kt2 + 2,
                                      jj * 128:(jj + 1) * 128],
                                x83[:, 2 * kt2:2 * kt2 + 2,
                                    tt * 512:(tt + 1) * 512],
                                start=(kt2 == 0), stop=(kt2 == 7),
                                perf_mode=mybir.MatmulPerfMode.DoubleRow)
                    return last
                pe_f = P.op("tensor", mmfc,
                            waits=[(sV, fc1_vq[jsl]), (sV, v_x8f),
                                   bw("P0"), bw("P1"), bw("P2"), bw("P3")],
                            inc=sPE)
                fc1_pe[jsl] = pe_f
                for tt in range(4):
                    par = tt % 2
                    gqc, gsv = rb_guard.get(par, (None, None))
                    a_r = P.op("scalar", lambda e, tt=tt, par=par: e.activation(
                        rbuf[:, par * 512:(par + 1) * 512], PB[tt][:, 0:512],
                        AF.Relu),
                        waits=[(sPE, pe_f), (qDH[par], gqc), (sV, gsv)], inc=sA)
                    bankw[f"P{tt}"] = (sA, a_r)
                    v_r4 = P.op("vector", lambda e, par=par: e.tensor_reduce(
                        red_t[:, 1:2], rbuf[:, par * 512:(par + 1) * 512],
                        AXX, ALU.max), waits=[(sA, a_r)], inc=sV)
                    g_m = P.op("vector", lambda e: e.tensor_max(
                        xmax_c[:, 3:4], xmax_c[:, 3:4], red_t[:, 1:2]),
                        waits=[(sV, v_r4)], inc=sV)
                    q_s = P.op("scalar", lambda e, jsl=jsl, jj=jj, tt=tt, par=par:
                               e.dma_start(out=d_h_bl[jsl * 2 + jj, tt],
                                           in_=rbuf[:, par * 512:(par + 1) * 512]),
                               inc=qDH[par], inc_k=16)
                    rb_guard[par] = (q_s, g_m)
        fc1_dh = (qDH[0].n, qDH[1].n)
        fc1_g = sV.n
        fc1_pe_fin = sPE.n

        pe_rm = P.op("tensor", lambda e: e.transpose(
            PB2[0][0:1, 0:128], xmax_c[:, 3:4], identf[:]),
            waits=[(sV, fc1_g), bw("P2_0")], inc=sPE)
        v_rm = P.op("vector", lambda e: e.tensor_reduce(
            one1[0:1, O_RM:O_RM + 1], PB2[0][0:1, 0:128], AXX, ALU.max),
            waits=[(sPE, pe_rm)], inc=sV)
        bankw["P2_0"] = (sV, v_rm)
        v_h2a = P.op("vector", lambda e: e.tensor_mul(
            one1[0:1, O_H2L:O_H2L + 1], one1[0:1, O_RM:O_RM + 1],
            one1[0:1, O_C1:O_C1 + 1]), waits=[(sV, v_c1), (sV, v_rm)], inc=sV)
        v_h2l = P.op("vector", lambda e: e.tensor_scalar_mul(
            one1[0:1, O_H2L:O_H2L + 1], one1[0:1, O_H2L:O_H2L + 1],
            one1[0:1, O_H2L:O_H2L + 1]), waits=[(sV, v_h2a)], inc=sV)
        v_h2row = P.op("vector", lambda e: e.tensor_scalar_mul(
            cc1row[0:1, :], onesrow[0:1, :], one1[0:1, O_H2L:O_H2L + 1]),
            waits=[(sV, v_h2l)], inc=sV)
        qg_spst = p7_state["qg_sp"]

        def emit_sh_chain():
            shcol = scal[:, SC_SH:SC_SH + 1]
            v_shs = P.op("vector", lambda e: e.tensor_scalar(
                one1[0:1, O_T2:O_T2 + 1], one1[0:1, O_T2:O_T2 + 1],
                1.0 / 448.0, 1e-6, ALU.mult, ALU.max),
                waits=[(qC1, qg_h2o)], inc=sV)
            v_sh = bcast_col(O_T2, SC_SH, v_shs)
            v_sh2 = P.op("vector", lambda e: e.tensor_scalar_mul(
                h2col[:, 0:1], shcol, 2.0), waits=[(sV, v_sh)], inc=sV)
            v_rc7 = P.op("vector", lambda e: e.reciprocal(
                h2col[:, 1:2], h2col[:, 0:1]), waits=[(sV, v_sh2)], inc=sV)
            v_c1s = P.op("vector", lambda e: e.tensor_scalar_mul(
                h2col[:, 2:3], scal[:, SC_C1:SC_C1 + 1],
                scal[:, SC_C1:SC_C1 + 1]),
                waits=[(sV, v_c1b)], inc=sV)
            v_f = P.op("vector", lambda e: e.tensor_mul(
                h2col[:, 2:3], h2col[:, 2:3], h2col[:, 1:2]),
                waits=[(sV, v_rc7), (sV, v_c1s)], inc=sV)
            P.op("scalar", lambda e: e.activation(
                h2col[:, 3:4], h2col[:, 2:3], AF.Ln), waits=[(sV, v_f)])
            a_sqf = P.op("scalar", lambda e: e.activation(
                scal[:, SC_SQF:SC_SQF + 1], h2col[:, 3:4], AF.Exp,
                scale=consts[:, C_P05:C_P05 + 1]), inc=sA)
            v_sh4 = P.op("vector", lambda e: e.tensor_scalar_mul(
                h2col[:, 4:5], shcol, 4.0),
                waits=[(sA, a_sqf), (sV, v_sprb)], inc=sV)
            v_c2 = P.op("vector", lambda e: e.tensor_mul(
                scal[:, SC_C2:SC_C2 + 1], h2col[:, 4:5],
                scal[:, SC_SPR:SC_SPR + 1]), waits=[(sV, v_sh4)], inc=sV)
            return a_sqf, v_c2

        esC.close()
        esP7.close()
        esW.close()

        esO = es.enter_context(ExitStack())
        sbO = mk_sb(esO)
        h8ab = [sbO("h8a", [128, 64 * 512], F8), sbO("h8b", [128, 64 * 512], F8)]
        hstg = sbO("hstg", [128, 6 * 512], BF16)
        sbst = sbO("sbst", [128, 2 * 512], BF16)

        esD = esO.enter_context(ExitStack())
        sbD = mk_sb(esD)
        sprev = sbD("sprev", [128, H * DV], BF16)
        woin = sbD("woin", [128, 16 * T], BF16)
        qg_spld = P.op("gpsimd", lambda e: e.dma_start(
            out=sprev[:], in_=d_sp[:]),
            waits=[(qSP2, qg_spst), (sPE, fc1_pe_fin), (sV, fc1_g),
                   (qDH[0], fc1_dh[0]), (qDH[1], fc1_dh[1])],
            inc=qSP2, inc_k=16)
        wo_mm_l = {}
        wo_ld = {}

        def emit_wosl_ld(ct):
            wp = ct % 2
            wo_ld[ct] = P.op("gpsimd", lambda e, ct=ct, wp=wp: e.dma_start(
                out=wosl2[wp][:], in_=wo_h[ct * 128:(ct + 1) * 128, :]),
                waits=[(sPE, wo_mm_l.get(ct - 2)
                        if ct >= 2 else fc1_pe_fin)],
                inc=qWO, inc_k=16)

        emit_wosl_ld(0)
        emit_wosl_ld(1)
        qg_h2s = P.op("gpsimd", lambda e: e.dma_start(
            out=cch_src[:], in_=cc1row[:]),
            waits=[(sV, v_h2row)], inc=qC1, inc_k=16)
        h2_n = P.op("gpsimd", lambda e: e.collective_compute(
            "AllReduce", ALU.max, replica_groups=[list(range(NCORE))],
            ins=[cch_src.opt()], outs=[cch_out.opt()]),
            waits=[(qC1, qg_h2s)], inc=sCC)
        qg_h2o = P.op("gpsimd", lambda e: e.dma_start(
            out=one1[0:1, O_T2:O_T2 + 1], in_=cch_out[0:1, 0:1]),
            waits=[(sCC, h2_n)], inc=qC1, inc_k=16)
        v_spb = None
        og16 = sbD("og16", [128, 16 * DV], BF16)
        qk_s = sbD("qk_s", [128, T], BF16)
        af = sbD("af", [128, DV], BF16)
        afs = sbD("afs", [128, DV], BF16)
        dmp = sbD("dmp", [128, DV], F32)
        wosl2 = [sbD("wosl0", [128, 16 * 512], BF16),
                 sbD("wosl1", [128, 16 * 512], BF16)]

        p8_v = None
        a_si_prev = None
        pe_tr_prev = None
        p8_vrb = {}
        a_sqf, v_c2 = None, None
        for h in range(H):
            if h == 1:
                a_sqf, v_c2 = emit_sh_chain()
            qc_og = P.op("scalar", lambda e, h=h: e.dma_start(
                out=og16[:], in_=d_og_r[h]),
                waits=[(qSPo, spill_og[h]), (sV, p8_v)] +
                ([(sV, fc1_g), (sPE, fc1_pe_fin),
                  (qDH[0], fc1_dh[0]), (qDH[1], fc1_dh[1])]
                 if h == 0 else []),
                inc=qRO, inc_k=16)
            qc_qk = P.op("scalar", lambda e, h=h: e.dma_start(
                out=qk_s[:], in_=d_qk_r[h]),
                waits=[(qSPq, spill_qk[h])], inc=qRQ, inc_k=16)
            for tt in range(16):
                key = f"P{tt % 2}"
                pe_c = P.op("tensor", lambda e, h=h, tt=tt: e.matmul(
                    PB[tt % 2][:, 0:DV], qk_s[:, tt * 128:(tt + 1) * 128],
                    sprev[:, h * DV:(h + 1) * DV], start=True, stop=True),
                    waits=[(qRQ, qc_qk), (qSP2, qg_spld), bw(key)],
                    inc=sPE)
                v_o = P.op("vector", lambda e, tt=tt: e.tensor_add(
                    og16[:, tt * DV:(tt + 1) * DV], PB[tt % 2][:, 0:DV],
                    og16[:, tt * DV:(tt + 1) * DV]),
                    waits=[(sPE, pe_c), (qRO, qc_og if tt == 0 else None)],
                    inc=sV)
                bankw[key] = (sV, v_o)
                a_q8 = P.op("scalar", lambda e, tt=tt: e.activation(
                    dmp[:], og16[:, tt * DV:(tt + 1) * DV], AF.Square),
                    waits=[(sV, v_o), (sV, p8_vrb.get((tt - 1) % 16))], inc=sA)
                p8_vrb[tt] = P.op("vector", lambda e, tt=tt: e.tensor_reduce(
                    stat_b[:, tt:tt + 1], dmp[:], AXX, ALU.add),
                    waits=[(sA, a_q8)], inc=sV)
                P.op("vector", lambda e, tt=tt: e.tensor_reduce(
                    stat_a[:, tt:tt + 1], og16[:, tt * DV:(tt + 1) * DV],
                    AXX, ALU.add), inc=sV if tt == 15 else None)
            v_st = sV.n
            a_st = sA.n
            v_8a = P.op("vector", lambda e: e.tensor_scalar_mul(
                stat_a[:, 0:16], stat_a[:, 0:16], 1.0 / DV),
                waits=[(sA, a_st), (sV, v_st)], inc=sV)
            v_8b = P.op("vector", lambda e: e.tensor_scalar_mul(
                stat_b[:, 0:16], stat_b[:, 0:16], 1.0 / DV),
                waits=[(sV, p8_vrb[15])], inc=sV)
            v_8c = P.op("vector", lambda e: e.tensor_mul(
                stat_c[:, 0:16], stat_a[:, 0:16],
                stat_a[:, 0:16] if False else stat_a[:, 0:16]),
                waits=[(sV, v_8a)], inc=sV)
            v_va = P.op("vector", lambda e: e.tensor_sub(
                stat_c[:, 0:16], stat_b[:, 0:16], stat_c[:, 0:16]),
                waits=[(sV, v_8b), (sV, v_8c)], inc=sV)
            P.op("scalar", lambda e: e.activation(
                stat_c[:, 0:16], stat_c[:, 0:16], AF.Ln,
                bias=consts[:, C_EPS:C_EPS + 1]), waits=[(sV, v_va)])
            a_rs = P.op("scalar", lambda e: e.activation(
                stat_c[:, 0:16], stat_c[:, 0:16], AF.Exp,
                scale=consts[:, C_M05:C_M05 + 1]), inc=sA)
            for tt in range(16):
                v_n = P.op("vector", lambda e, tt=tt: e.tensor_scalar(
                    af[:], og16[:, tt * DV:(tt + 1) * DV],
                    stat_a[:, tt:tt + 1], stat_c[:, tt:tt + 1],
                    ALU.subtract, ALU.mult),
                    waits=[(sA, a_rs), (sA, a_si_prev)], inc=sV)
                v_nw = P.op("vector", lambda e: e.tensor_mul(
                    af[:], af[:], lnwb[:]), waits=[(sV, v_n)], inc=sV)
                v_af = P.op("vector", lambda e: e.tensor_add(
                    af[:], af[:], lnbb[:]), waits=[(sV, v_nw)], inc=sV)
                a_si = P.op("scalar", lambda e: e.activation(
                    afs[:], af[:], AF.Silu),
                    waits=[(sV, v_af), (sPE, pe_tr_prev)], inc=sA)
                a_si_prev = a_si
                for eh in range(2):
                    key = f"Pb_{eh}"
                    pe_t = P.op("tensor", lambda e, eh=eh: e.transpose(
                        PBb[eh][0:128, 0:128], afs[:, eh * 128:(eh + 1) * 128],
                        identb[:]), waits=[(sA, a_si), bw(key)], inc=sPE)
                    pe_tr_prev = pe_t
                    bankw[key] = (sV, P.op(
                        "vector", lambda e, h=h, tt=tt, eh=eh: e.tensor_copy(
                            woin[:, (2 * h + eh) * T + tt * 128:
                                 (2 * h + eh) * T + (tt + 1) * 128],
                            PBb[eh][0:128, 0:128]), waits=[(sPE, pe_t)], inc=sV))
            p8_v = sV.n
        p8_done = p8_v

        pe_wo_last = None
        sb_st = {}
        sb_prev = {}
        for ct in range(4):
            qg_wo = wo_ld[ct]
            woslb = wosl2[ct % 2]
            for tt in range(16):
                key = f"P{tt % 2}"

                def mmwo(e, ct=ct, tt=tt, woslb=woslb):
                    for vd in range(16):
                        last = e.matmul(
                            PB[tt % 2][:, 0:512],
                            woin[:, vd * T + tt * 128:vd * T + (tt + 1) * 128],
                            woslb[:, vd * 512:(vd + 1) * 512],
                            start=(vd == 0), stop=(vd == 15))
                    return last
                pe_wo = P.op("tensor", mmwo,
                             waits=[(qWO, qg_wo), (sV, p8_done), bw(key)], inc=sPE)
                if tt == 15:
                    wo_mm_l[ct] = pe_wo
                    if ct + 2 < 4:
                        emit_wosl_ld(ct + 2)
                pe_wo_last = pe_wo
                par = (ct * 16 + tt) % 2
                ev = P.op("vector", lambda e, par=par, tt=tt: e.tensor_copy(
                    sbst[:, par * 512:(par + 1) * 512], PB[tt % 2][:, 0:512]),
                    waits=[(sPE, pe_wo), (qSB[par], sb_prev.get(par))], inc=sV)
                bankw[key] = (sV, ev)
                sb_st[(tt, ct)] = P.op("sync", lambda e, ct=ct, tt=tt, par=par:
                                       e.dma_start(
                    out=d_sbf[:, tt * T + ct * 512:tt * T + (ct + 1) * 512],
                    in_=sbst[:, par * 512:(par + 1) * 512]),
                    waits=[(sV, ev)], inc=qSB[par], inc_k=16)
                sb_prev[par] = sb_st[(tt, ct)]
        p9_v = sV.n
        pe_wo_fin = pe_wo_last

        esD.close()

        esE = esO.enter_context(ExitStack())
        sbE = mk_sb(esE)
        w8ps2 = [sbE("w8ps0", [128, 64 * 512], F8),
                 sbE("w8ps1", [128, 64 * 512], F8)]
        gf = sbE("gf", [128, 512], F32)
        xf = [sbE("xf0", [128, 512], F32), sbE("xf1", [128, 512], F32)]
        of = [sbE("of0", [128, 512], F32), sbE("of1", [128, 512], F32)]
        sbl = [sbE("sbl0", [128, 512], BF16), sbE("sbl1", [128, 512], BF16)]

        d_h_ld = d_h.rearrange("(jt tt p) q -> jt tt p q", tt=4, p=128)

        hstg_a = {}
        v_m_prev = None
        last_dma = {}
        last_vo = {}
        last_vm = {}
        tg_mm_last = {}
        w8_mm_last = {}
        h8_done = {}
        hld_seq = [0]
        pe_last = None

        hb_pend = []

        def emit_h8_load(tg, jt):
            par6 = hld_seq[0] % 6
            hld_seq[0] += 1
            first = (tg == 0 and jt == 0)
            qs_hld = P.op("scalar", lambda e, jt=jt, tg=tg, par6=par6:
                          e.dma_start(
                out=hstg[:, par6 * 512:(par6 + 1) * 512],
                in_=d_h_ld[jt, tg]),
                waits=[(qDH[0], fc1_dh[0]), (qDH[1], fc1_dh[1]),
                       (sA, hstg_a.get(par6))] +
                ([(sPE, fc1_pe_fin), (sV, fc1_g), (qSP2, qg_spst)]
                 if first else []),
                inc=qH6[par6], inc_k=16)
            hb_pend.append((tg, jt, par6, qs_hld, first))

        def emit_h8_square():
            tg, jt, par6, qs_hld, first = hb_pend.pop(0)
            h8b_ = h8ab[tg % 2]
            hstg_a[par6] = P.op("scalar", lambda e, jt=jt, par6=par6,
                                h8b_=h8b_: e.activation(
                h8b_[:, jt * 512:(jt + 1) * 512],
                hstg[:, par6 * 512:(par6 + 1) * 512],
                AF.Square, scale=scal[:, SC_SQF:SC_SQF + 1]),
                waits=[(qH6[par6], qs_hld),
                       (sA, a_sqf if first else None),
                       (sPE, tg_mm_last.get(tg - 2) if jt == 0 else None)],
                inc=sA)

        def emit_h8_build(tg, jlo, jhi):
            for jt in range(jlo, jhi):
                emit_h8_load(tg, jt)
                if len(hb_pend) > 5:
                    emit_h8_square()
            if jhi == 64:
                while hb_pend:
                    emit_h8_square()
                h8_done[tg] = sA.n

        emit_h8_build(0, 0, 64)
        for tg in range(4):
            h8b_ = h8ab[tg % 2]
            for ct in range(4):
                if tg < 3:
                    emit_h8_build(tg + 1, ct * 16, (ct + 1) * 16)
                wpar = (tg * 4 + ct) % 2
                w8psb = w8ps2[wpar]
                qg_w8 = P.op("gpsimd", lambda e, ct=ct, w8psb=w8psb:
                             e.dma_start(
                    out=w8psb[:], in_=d_w8p[ct * 128:(ct + 1) * 128, :]),
                    waits=[(qW8, w8p_qc if (tg == 0 and ct < 2) else 0),
                           (sPE, w8_mm_last.get(wpar)
                            if (tg > 0 or ct > 1) else pe_wo_fin)],
                    inc=qW2[wpar], inc_k=16)
                for tt in range(4):
                    tta = tg * 4 + tt
                    key = f"P{tt % 2}"

                    def mmf2(e, tt=tt, h8b_=h8b_, w8psb=w8psb):
                        h83 = h8b_[:].rearrange("p (j q) -> p j q", j=64)
                        w83 = w8psb[:].rearrange("p (j q) -> p j q", j=64)
                        for jt2 in range(32):
                            last = e.matmul(
                                PB[tt % 2][:, 0:512],
                                h83[:, 2 * jt2:2 * jt2 + 2,
                                    tt * 128:(tt + 1) * 128],
                                w83[:, 2 * jt2:2 * jt2 + 2, :],
                                start=(jt2 == 0), stop=(jt2 == 31),
                                perf_mode=mybir.MatmulPerfMode.DoubleRow)
                        return last
                    pe_f2 = P.op("tensor", mmf2,
                                 waits=[(qW2[wpar], qg_w8),
                                        (sA, h8_done[tg]), bw(key)],
                                 inc=sPE)
                    pe_last = pe_f2
                    if tt == 3:
                        w8_mm_last[wpar] = pe_f2
                        tg_mm_last[tg] = pe_f2
                    a_gf = P.op("scalar", lambda e, tt=tt: e.activation(
                        gf[:], PB[tt % 2][:, 0:512], AF.Silu,
                        scale=scal[:, SC_C2:SC_C2 + 1]),
                        waits=[(sPE, pe_f2), (sV, v_c2), (sV, v_m_prev)],
                        inc=sA)
                    bankw[key] = (sA, a_gf)
                    par = tt % 2
                    qs_xf = P.op("sync", lambda e, tta=tta, ct=ct, par=par:
                                 e.dma_start(
                        out=xf[par][:],
                        in_=x_h[tta * 128:(tta + 1) * 128,
                                ct * 512:(ct + 1) * 512]),
                        waits=[(sV, last_vo.get(par))] +
                        ([(sPE, pe_wo_fin)] if (tg == 0 and ct == 0 and tt < 2)
                         else []),
                        inc=qXF[par], inc_k=16)
                    qs_sb = P.op("sync", lambda e, tta=tta, ct=ct, par=par:
                                 e.dma_start(
                        out=sbl[par][:],
                        in_=d_sbf[:, tta * T + ct * 512:
                                  tta * T + (ct + 1) * 512]),
                        waits=[(qSB[(ct * 16 + tta) % 2], sb_st[(tta, ct)]),
                               (sV, last_vm.get(par))] +
                        ([(sPE, pe_wo_fin)] if (tg == 0 and ct == 0 and tt < 2)
                         else []),
                        inc=qSBL[par], inc_k=16)
                    v_m = P.op("vector", lambda e, par=par: e.tensor_mul(
                        of[par][:], sbl[par][:], gf[:]),
                        waits=[(sA, a_gf), (qXF[par], qs_xf),
                               (qSBL[par], qs_sb),
                               (qY[par], last_dma.get(par))], inc=sV)
                    v_m_prev = v_m
                    last_vm[par] = v_m
                    v_o = P.op("vector", lambda e, par=par: e.tensor_add(
                        of[par][:], of[par][:], xf[par][:]),
                        waits=[(sV, v_m)], inc=sV)
                    last_vo[par] = v_o
                    last_dma[par] = P.op("sync", lambda e, tta=tta, ct=ct,
                                         par=par: e.dma_start(
                        out=y_h[tta * 128:(tta + 1) * 128,
                                ct * 512:(ct + 1) * 512],
                        in_=of[par][:]), waits=[(sV, v_o)],
                        inc=qY[par], inc_k=16)
        if debug:
            P.op("gpsimd", lambda e: e.dma_start(out=t_one1[:], in_=one1[:]),
                 waits=[(sV, v_c2)], inc=qG, inc_k=16)
            P.op("gpsimd", lambda e: e.dma_start(out=t_scal[:], in_=scal[:]),
                 inc=qG, inc_k=16)
            P.op("gpsimd", lambda e: e.dma_start(out=t_xn[:], in_=d_xn[:, :]),
                 inc=qG, inc_k=16)
            P.op("gpsimd", lambda e: e.dma_start(out=t_og[:], in_=d_og[:, :]),
                 inc=qG, inc_k=16)
            P.op("gpsimd", lambda e: e.dma_start(out=t_qk[:], in_=d_qk[:, :]),
                 inc=qG, inc_k=16)
            P.op("gpsimd", lambda e: e.dma_start(out=t_h[:], in_=d_h[0:256, :]),
                 inc=qG, inc_k=16)
            P.op("gpsimd", lambda e: e.dma_start(out=t_cc2[:], in_=cc2_out[:, :]),
                 inc=qG, inc_k=16)
            P.op("gpsimd", lambda e: e.dma_start(out=t_sbf[:], in_=d_sbf[:]),
                 waits=[(qSB[0], qSB[0].n), (qSB[1], qSB[1].n)],
                 inc=qG, inc_k=16)
            P.op("gpsimd", lambda e: e.dma_start(out=t_sprev[:], in_=sprev[:]),
                 waits=[(sV, v_spb)], inc=qG, inc_k=16)
            P.op("gpsimd", lambda e: e.dma_start(out=t_sdump[:], in_=sdump[:]),
                 inc=qG, inc_k=16)
        esE.close()
        esO.close()

        fin_waits = [(c, c.n) for c in sems
                     if c not in (sPE, sV, sA, sG, sCC) and c.n > 0]
        P.op("sync", lambda e: None, waits=fin_waits)

        ser_sems = {}
        for en in ser_engines:
            ser_sems[en] = es.enter_context(nc.semaphore(f"ser_{en}"))
        blk = es.enter_context(nc.Block())
        blk.sync(lambda e: P.replay("sync", e, ser_sems.get("sync")))
        blk.scalar(lambda e: P.replay("scalar", e, ser_sems.get("scalar")))
        blk.vector(lambda e: P.replay("vector", e, ser_sems.get("vector")))
        blk.tensor(lambda e: P.replay("tensor", e, ser_sems.get("tensor")))
        blk.gpsimd(lambda e: P.replay("gpsimd", e, ser_sems.get("gpsimd")))

    return nc


_NC_CACHE = {}


def kernel(L0_attn_hidden_states, L0_attn_Wq, L0_attn_Wk, L0_attn_Wv,
           L0_attn_Wg, L0_attn_Wgk1, L0_attn_Wgk2, L0_attn_bgk2, L0_attn_Wo,
           L0_attn_ln_w, L0_attn_ln_b, L0_ffn_w_fc, L0_ffn_w_proj):
    x = np.ascontiguousarray(np.asarray(L0_attn_hidden_states, np.float32))
    B, S, _ = x.shape
    wq0 = np.asarray(L0_attn_Wq, np.float32)
    wk0 = np.asarray(L0_attn_Wk, np.float32)
    wv = np.asarray(L0_attn_Wv, np.float32)
    wg = np.asarray(L0_attn_Wg, np.float32)
    wq = np.ascontiguousarray(
        wq0.reshape(16, 128, 8, 128).transpose(2, 1, 0, 3).reshape(1024, 2048))
    wk = np.ascontiguousarray(
        wk0.reshape(16, 128, 8, 128).transpose(2, 1, 0, 3).reshape(1024, 2048))
    wvg0 = np.concatenate(
        [np.concatenate([wv[:, h * DV:(h + 1) * DV],
                         wg[:, h * DV:(h + 1) * DV]], axis=1)
         for h in range(H)], axis=1)
    wvg = np.ascontiguousarray(
        wvg0.reshape(16, 128, 8, 512).transpose(2, 1, 0, 3).reshape(1024, 8192))
    wgk1 = np.ascontiguousarray(np.asarray(L0_attn_Wgk1, np.float32))
    wgk2 = np.ascontiguousarray(np.asarray(L0_attn_Wgk2, np.float32))
    bg = np.ascontiguousarray(
        np.asarray(L0_attn_bgk2, np.float32).reshape(H, 128).T)
    wo = np.ascontiguousarray(
        np.asarray(L0_attn_Wo, np.float32).reshape(16, 128, 4, 512)
        .transpose(2, 1, 0, 3).reshape(512, 8192))
    lnw = np.ascontiguousarray(np.asarray(L0_attn_ln_w, np.float32)[None, :])
    lnb = np.ascontiguousarray(np.asarray(L0_attn_ln_b, np.float32)[None, :])
    wfcT = np.ascontiguousarray(np.asarray(L0_ffn_w_fc, np.float32).T)
    wprT = np.ascontiguousarray(np.asarray(L0_ffn_w_proj, np.float32).T)

    identb = np.eye(128, dtype=ml_dtypes.bfloat16)
    identf = np.eye(128, dtype=np.float32)
    n_i = np.arange(64)
    m64 = ((n_i[None, :] >= n_i[:, None]) &
           (n_i[None, :] // 16 == n_i[:, None] // 16)).astype(np.float32)
    cm = np.ones((128, T), np.float32)
    cm[:, ::CH] = 0.0
    cmask = cm.astype(ml_dtypes.bfloat16)
    consts = np.zeros((128, 8), np.float32)
    consts[:, C_EPS] = EPS
    consts[:, C_M05] = -0.5
    consts[:, C_M1] = -1.0
    consts[:, C_P05] = 0.5
    onesrow = np.ones((1, 128), np.float32)

    if "prog" not in _NC_CACHE:
        _NC_CACHE["prog"] = build_program()
    nc = _NC_CACHE["prog"]

    in_maps = []
    for c in range(NCORE):
        b, s = c // 2, c % 2
        smask = np.zeros((128, 8), np.float32)
        if s == 1:
            smask[:, c - 1] = 1.0
        in_maps.append(dict(
            x_own=np.ascontiguousarray(x[b, s * T:(s + 1) * T, :]),
            Wq=wq, Wk=wk, Wvg=wvg, Wgk1=wgk1, Wgk2=wgk2, bgk2c=bg, Wo=wo,
            ln_w=lnw, ln_b=lnb, wfcT=wfcT, wprT=wprT,
            identb=identb, identf=identf, m64=m64, cmask=cmask,
            consts=consts, onesrow=onesrow, smask=smask))
    res = run_bass_kernel_spmd(nc, in_maps, core_ids=list(range(NCORE)))
    out = np.empty((B, S, D), np.float32)
    for c in range(NCORE):
        b, s = c // 2, c % 2
        out[b, s * T:(s + 1) * T, :] = res.results[c]["y"]
    return out


if __name__ == "__main__":
    build_program()
    print("build ok")
